# version 01
# baseline (speedup 1.0000x reference)
import numpy as np
from contextlib import ExitStack
import concourse.bass as bass
import concourse.mybir as mybir
from concourse.bass_utils import run_bass_kernel_spmd
from concourse.bass import broadcast_tensor_aps

F32 = mybir.dt.float32
BF16 = mybir.dt.bfloat16
I32 = mybir.dt.int32
AF = mybir.ActivationFunctionType
ALU = mybir.AluOpType
AX = mybir.AxisListType

D = 1024
S = 2048
NSEQ = 2
T = NSEQ * S
NCH = 8
GRP = 512
NG = S // GRP
H = 16
DH = 64
NPAIR = 8
LAT = 256
SCALE = 0.125
ESHIFT = 16.0
EPS = 1e-5
NJC = S // 128

SB_A0, SB_F0, SB_A1, SB_F1 = 0, 1, 2, 3

MODE = "sb4"


ROLE_ATTRS = [("sync", "sync"), ("tensor", "tensor"), ("vector", "vector"),
              ("scalar", "scalar"), ("gpsimd", "gpsimd")]


class Ctx:
    def __init__(self, nc, sems):
        self.nc = nc
        self.sem = sems
        self.cnt = {k: 0 for k in sems}

    def phase(self, name, flow):
        base = dict(self.cnt)

        def fenced(r):
            for s in ("pe", "act", "dve", "pool", "dx", "dw"):
                r.wait(r.role, s)
            flow(r)

        with self.nc.Block(name) as blk:
            for role, attr in ROLE_ATTRS:
                def mk(role):
                    def body(eng):
                        self.cnt = dict(base)
                        fenced(R(self, role, eng))
                    return body
                getattr(blk, attr)(mk(role))


class R:
    __slots__ = ("ctx", "role", "eng")

    def __init__(self, ctx, role, eng):
        self.ctx = ctx
        self.role = role
        self.eng = eng

    def _fin(self, inst, inc):
        if inc is not None:
            name, n = inc
            if inst is not None:
                inst.then_inc(self.ctx.sem[name], n)
            self.ctx.cnt[name] += n

    def wait(self, role, sem, val=None):
        v = self.ctx.cnt[sem] if val is None else val
        if v <= 0:
            return
        if self.role == role:
            self.eng.wait_ge(self.ctx.sem[sem], v)

    def mm(self, out, lhsT, rhs, start=True, stop=True, tile_position=None, inc=None):
        inst = None
        if self.role == "tensor":
            inst = self.eng.matmul(out, lhsT, rhs, start=start, stop=stop,
                                   tile_position=tile_position)
        self._fin(inst, inc)

    def act(self, out, in_, func, bias, scale=1.0, inc=None):
        inst = None
        if self.role == "scalar":
            inst = self.eng.activation(out, in_, func, bias=bias, scale=scale)
        self._fin(inst, inc)

    def tt(self, out, in0, in1, op, inc=None):
        inst = None
        if self.role == "vector":
            inst = self.eng.tensor_tensor(out, in0, in1, op=op)
        self._fin(inst, inc)

    def ttb(self, out, in0, in1b, op, inc=None):
        inst = None
        if self.role == "vector":
            b = broadcast_tensor_aps(in1b, in0)[0]
            inst = self.eng.tensor_tensor(out, in0, b, op=op)
        self._fin(inst, inc)

    def ts(self, out, in0, s1, s2, op0, op1=None, inc=None):
        inst = None
        if self.role == "vector":
            if op1 is None:
                inst = self.eng.tensor_scalar(out, in0, s1, None, op0=op0)
            else:
                inst = self.eng.tensor_scalar(out, in0, s1, s2, op0=op0, op1=op1)
        self._fin(inst, inc)

    def cp(self, out, in_, inc=None):
        inst = None
        if self.role == "vector":
            inst = self.eng.tensor_copy(out, in_)
        self._fin(inst, inc)

    def red(self, out, in_, op, absval=False, inc=None):
        inst = None
        if self.role == "vector":
            inst = self.eng.tensor_reduce(out, in_, AX.X, op,
                                          apply_absolute_value=absval)
        self._fin(inst, inc)

    def recip(self, out, in_, inc=None):
        inst = None
        if self.role == "vector":
            inst = self.eng.reciprocal(out, in_)
        self._fin(inst, inc)

    def ms(self, ap, val, inc=None):
        inst = None
        if self.role == "gpsimd":
            inst = self.eng.memset(ap, val)
        self._fin(inst, inc)

    def iota(self, ap, pattern, cm, inc=None):
        inst = None
        if self.role == "gpsimd":
            inst = self.eng.iota(ap, pattern, channel_multiplier=cm,
                                 allow_small_or_imprecise_dtypes=True)
        self._fin(inst, inc)

    def pow_tt(self, out, in0, in1, inc=None):
        inst = None
        if self.role == "gpsimd":
            inst = self.eng.tensor_tensor(out, in0, in1, op=ALU.pow)
        self._fin(inst, inc)

    def dma(self, role, out, in_, inc=None):
        inst = None
        if self.role == role:
            inst = self.eng.dma_start(out=out, in_=in_)
        self._fin(inst, inc)


class NS:
    pass


def alloc_buffers(nc, stack):
    E = stack.enter_context
    B = NS()
    sb = nc.sbuf_tensor
    B.xb = E(sb("xb", [128, NCH, GRP], BF16))
    B.xres = E(sb("xres", [128, NCH, GRP], F32))
    B.xn = E(sb("xn", [128, NCH, S + 1], BF16))
    B.kT = E(sb("kT", [128, NPAIR, S], BF16))
    B.v_s = E(sb("v_s", [128, NJC, H * 65], BF16))
    B.ctab = E(sb("ctab", [128, 2, S], BF16))
    B.qp = E(sb("qp", [128, GRP], BF16))
    B.qtmp = E(sb("qtmp", [128, GRP], BF16))
    B.rtmp = E(sb("rtmp", [128, S], BF16))
    B.expb = E(sb("expb", [128, 4, 1024], BF16))
    B.avu = E(sb("avu", [128, 2, GRP], BF16))
    B.dsb = E(sb("dsb", [65, 2, GRP], F32))
    B.selb = E(sb("selb", [65, 64], F32))
    B.ao = E(sb("ao", [128, NPAIR, GRP], BF16))
    B.yb = E(sb("yb", [128, NCH, GRP], BF16))
    B.t1 = E(sb("t1", [128, NCH, GRP], BF16))
    B.t2 = E(sb("t2", [128, NCH, GRP], BF16))
    B.wsl = E(sb("wsl", [128, 4, NCH, 128], BF16))
    B.Pswp = E(sb("Pswp", [128, 128], BF16))
    B.I64 = E(sb("I64", [64, 64], BF16))
    B.Bv = E(sb("Bv", [128, D], BF16))
    B.rbs = E(sb("rbs", [128, GRP], BF16))
    B.abs_ = E(sb("abs_", [128, GRP], BF16))
    B.cols = E(sb("cols", [128, 128], F32))
    B.rows = E(sb("rows", [1, 3, GRP], F32))
    B.onesb = E(sb("onesb", [1, 128], BF16))
    B.onesf = E(sb("onesf", [1, 128], F32))
    B.negf = E(sb("negf", [1, 128], F32))
    B.onesD = E(sb("onesD", [128, 1], BF16))
    B.onesP = E(sb("onesP", [128, 1], F32))
    B.dcol = E(sb("dcol", [128, 1], F32))
    B.e11 = E(sb("e11", [1, 1], F32))
    B.z11 = E(sb("z11", [1, 1], F32))
    B.scrI = E(sb("scrI", [128, GRP], I32))
    B.latT = B.ctab
    B.cosT = B.ctab[:, 0, :]
    B.sinT = B.ctab[:, 1, :]
    B.fr = B.ao
    B.fk = B.expb[:].rearrange("p a b -> p (a b)").rearrange("p (c t) -> p c t", t=GRP)
    B.wrhs = B.t2
    B.scrA = B.xres[:, 0, :]
    B.scrB = B.xres[:, 1, :]
    B.scrC = B.xres[:, 2, :]
    B.scrW = B.xres[:, 4:6, :].rearrange("p c t -> p (c t)")
    B.scrW2 = B.xres[:, 6:8, :].rearrange("p c t -> p (c t)")
    B.brow = B.rows[0:1, 0:2, :].rearrange("a c t -> a (c t)")
    ps = nc.psum_tensor
    B.pSC = E(ps("pSC", [128, 1024], F32))
    B.pSC2 = E(ps("pSC2", [128, 1024], F32))
    B.pQ = E(ps("pQ", [128, 512], F32))
    B.pM = E(ps("pM", [128, 512], F32))
    B.pavA = E(ps("pavA", [128, 512], F32))
    B.pavB = E(ps("pavB", [128, 512], F32))
    B.col = {
        "bq0": 0, "bk0": 8, "bo0": 16, "fb": 24, "bq1": 32, "bk1": 40,
        "bo1": 48, "mur": 56, "muk": 64, "murc": 72, "mukc": 80,
        "bl0": 88,
        "alpha": 90, "eshift": 91, "zero": 92, "delta": 93, "absacc": 94,
        "pxorf": 102, "pf": 103,
    }
    return B


def ccol(B, name, w=1):
    o = B.col[name]
    return B.cols[:, o:o + w]


def ccol_off(name):
    return {"bq0": 0, "bk0": 8, "bo0": 16, "fb": 24, "bq1": 32, "bk1": 40,
            "bo1": 48, "mur": 56, "muk": 64, "murc": 72, "mukc": 80,
            "bl0": 88, "alpha": 90, "eshift": 91, "zero": 92, "delta": 93,
            "absacc": 94, "pxorf": 102, "pf": 103}[name]


def B_cols_ap(B):
    return B.cols[:]


def emit_ln(ctx, r, B, src3, dst3, resid=None, store=None, loadres=None):
    r.wait("vector", "dw")
    r.wait("vector", "pe")
    r.tt(B.t1[:], src3, src3, op=ALU.mult, inc=("dve", 1))
    if loadres is not None:
        r.wait("sync", "dve")
        r.dma("sync", out=B.xres[:], in_=loadres, inc=("dx", 16))
    r.wait("tensor", "dw")
    for c in range(NCH):
        r.mm(B.pQ[0:1, :], B.onesD[:], src3[:, c, :], start=(c == 0), stop=(c == NCH - 1),
             inc=("pe", 1) if c == NCH - 1 else None)
    r.wait("tensor", "dve")
    for c in range(NCH):
        r.mm(B.pM[0:1, :], B.onesD[:], B.t1[:, c, :], start=(c == 0), stop=(c == NCH - 1),
             inc=("pe", 1) if c == NCH - 1 else None)
    r.wait("vector", "pe")
    r.cp(B.rows[:, 0, :], B.pQ[0:1, :])
    r.cp(B.rows[:, 1, :], B.pM[0:1, :])
    r.tt(B.rows[:, 2, :], B.rows[:, 0, :], B.rows[:, 0, :], op=ALU.mult)
    r.tt(B.rows[:, 2, :], B.rows[:, 1, :], B.rows[:, 2, :], op=ALU.subtract, inc=("dve", 1))
    r.wait("scalar", "dve")
    r.act(B.rows[:, 2, :], B.rows[:, 2, :], AF.Ln, bias=B.e11[:], scale=1.0)
    r.act(B.rows[:, 2, :], B.rows[:, 2, :], AF.Exp, bias=B.z11[:], scale=-0.5, inc=("act", 1))
    r.wait("vector", "act")
    r.tt(B.rows[:, 1, :], B.rows[:, 0, :], B.rows[:, 2, :], op=ALU.mult, inc=("dve", 1))
    r.wait("tensor", "act")
    r.wait("tensor", "dve")
    r.mm(B.pSC[:, 0:512], B.onesf[:], B.rows[:, 2, :], inc=("pe", 1))
    r.mm(B.pSC[:, 512:1024], B.negf[:], B.rows[:, 1, :], inc=("pe", 1))
    r.wait("vector", "pe")
    r.cp(B.rbs[:], B.pSC[:, 0:512])
    r.cp(B.abs_[:], B.pSC[:, 512:1024], inc=("dve", 1))
    rb = B.rbs[:].rearrange("p (o t) -> p o t", o=1)
    ab = B.abs_[:].rearrange("p (o t) -> p o t", o=1)
    r.ttb(B.t1[:], src3, rb, op=ALU.mult)
    if resid is None:
        r.ttb(dst3, B.t1[:], ab, op=ALU.add, inc=("dve", 1))
    else:
        r.ttb(B.t2[:], B.t1[:], ab, op=ALU.add)
        r.wait("vector", "dx")
        r.tt(B.xres[:], B.xres[:], B.t2[:], op=ALU.add, inc=("dve", 1))
        if store is not None:
            r.wait("sync", "dve")
            r.dma("sync", out=store, in_=B.xres[:], inc=("dx", 16))


def dram_grp(dram, seq, g, w=GRP):
    c0 = seq * S + g * GRP
    xr = dram.rearrange("(c p) t -> p c t", p=128)
    return xr[:, :, c0:c0 + w]


def wslice_ap(wdram, m, kw=NCH):
    wr = wdram.rearrange("(c p) o -> p c o", p=128)
    return wr[:, :, m * 128:(m + 1) * 128]


def emit_kv_phase(ctx, B, seq, L):
    wq = "Wq0" if L == 0 else "wq1t"
    wk = "Wk0" if L == 0 else "wk1t"
    wv = "Wv0" if L == 0 else "wv1t"
    bkc = ccol(B, "bk0" if L == 0 else "bk1", 8)
    nkk = 2 if L == 0 else NCH

    def kv_flow(r):
        if L == 0:
            for m2 in range(2):
                slot = m2 % 4
                r.dma("gpsimd", out=B.wsl[:, slot, :, :], in_=wslice_ap(ctx.dram["Wl0"], m2),
                      inc=("dw", 16))
                r.wait("tensor", "dw")
                for g in range(NG):
                    ps = [B.pSC, B.pSC2][g % 2][:, 0:512]
                    r.wait("tensor", "act", ctx.cnt["act"])
                    for k in range(NCH):
                        r.mm(ps, B.wsl[:, slot, k, :], B.xn[:, k, 1 + g * GRP:1 + (g + 1) * GRP],
                             start=(k == 0), stop=(k == NCH - 1),
                             inc=("pe", 1) if k == NCH - 1 else None)
                    r.wait("scalar", "pe")
                    r.act(B.latT[:, m2, g * GRP:(g + 1) * GRP], ps, AF.Identity,
                          bias=ccol(B, "bl0", 2)[:, m2:m2 + 1], scale=1.0, inc=("act", 1))
        for m in range(NPAIR):
            slot = m % 4
            r.wait("gpsimd", "act")
            r.dma("gpsimd", out=B.wsl[:, slot, :, 0:128] if nkk == NCH else B.wsl[:, slot, 0:nkk, :],
                  in_=wslice_ap(ctx.dram[wk], m, nkk), inc=("dw", 16))
            r.wait("tensor", "dw")
            for g in range(NG):
                ps = [B.pSC, B.pSC2][g % 2][:, 0:512]
                r.wait("tensor", "act", ctx.cnt["act"])
                for k in range(nkk):
                    rhs = (B.latT[:, k, g * GRP:(g + 1) * GRP] if L == 0 else
                           B.xn[:, k, 1 + g * GRP:1 + (g + 1) * GRP])
                    r.mm(ps, B.wsl[:, slot, k, :], rhs, start=(k == 0), stop=(k == nkk - 1),
                         inc=("pe", 1) if k == nkk - 1 else None)
                r.wait("scalar", "pe")
                r.act(B.kT[:, m, g * GRP:(g + 1) * GRP], ps, AF.Identity,
                      bias=bkc[:, m:m + 1], scale=1.0, inc=("act", 1))
            if L == 1:
                row = B.kT[:, m, :]
                r.wait("tensor", "act")
                for gg in range(4):
                    ps = [B.pSC, B.pSC2][gg // 2][:, (gg % 2) * 512:(gg % 2) * 512 + 512]
                    r.mm(ps, B.Pswp[:], row[:, gg * 512:(gg + 1) * 512], inc=("pe", 1))
                r.wait("vector", "act")
                r.tt(B.rtmp[:], row, B.cosT[:], op=ALU.mult)
                r.wait("vector", "pe")
                for gg in range(4):
                    ps = [B.pSC, B.pSC2][gg // 2][:, (gg % 2) * 512:(gg % 2) * 512 + 512]
                    r.tt(row[:, gg * 512:(gg + 1) * 512], ps,
                         B.sinT[:, gg * 512:(gg + 1) * 512], op=ALU.mult)
                r.tt(row, row, B.rtmp[:], op=ALU.add, inc=("dve", 1))
                r.wait("tensor", "dve")
        for half in range(2):
            r.wait("gpsimd", "dve")
            r.wait("gpsimd", "act")
            r.wait("gpsimd", "pe")
            r.dma("gpsimd", out=B.wrhs[:, 0:nkk, :],
                  in_=ctx.dram[wv].rearrange("(c p) o -> p c o", p=128)[:, :, half * 512:(half + 1) * 512],
                  inc=("dw", 16))
            r.wait("tensor", "dw")
            for jc in range(NJC):
                ps = [B.pSC, B.pSC2][jc % 2][:, 0:512]
                r.wait("tensor", "dve", ctx.cnt["dve"])
                for k in range(nkk):
                    lhsT = (B.latT[:, k, jc * 128:(jc + 1) * 128] if L == 0 else
                            B.xn[:, k, 1 + jc * 128:1 + (jc + 1) * 128])
                    r.mm(ps, lhsT, B.wrhs[:, k, :], start=(k == 0), stop=(k == nkk - 1),
                         inc=("pe", 1) if k == nkk - 1 else None)
                r.wait("vector", "pe")
                vv = B.v_s[:, jc, :].rearrange("p (h c) -> p h c", c=65)
                bv = B.Bv[:, half * 512:(half + 1) * 512].rearrange("p (h c) -> p h c", c=64)
                r.tt(vv[:, half * 8:(half + 1) * 8, 0:64],
                     ps.rearrange("p (h c) -> p h c", c=64), bv, op=ALU.add, inc=("dve", 1))
    ctx.phase(f"kv_a{L}_s{seq}", kv_flow)


def emit_attn(ctx, B, seq, L, src, dst):
    wq = "Wq0" if L == 0 else "wq1t"
    wk = "Wk0" if L == 0 else "wk1t"
    wv = "Wv0" if L == 0 else "wv1t"
    wo = "Wo0" if L == 0 else "wo1t"
    bqc = ccol(B, "bq0" if L == 0 else "bq1", 8)
    bkc = ccol(B, "bk0" if L == 0 else "bk1", 8)
    boc = ccol(B, "bo0" if L == 0 else "bo1", 8)
    nkk = 2 if L == 0 else NCH
    bvrow = "bv0row" if L == 0 else "bv1row"

    if L == 1:
        emit_prep_tables(ctx, B, tag=f"_s{seq}")

    def bv_flow(r):
        r.wait("sync", "dx")
        r.wait("sync", "pe")
        r.wait("sync", "dve")
        r.dma("sync", out=B.brow, in_=ctx.dram[bvrow], inc=("dx", 16))
        r.wait("tensor", "dx")
        r.mm(B.pSC[:, 0:512], B.onesf[:], B.rows[0:1, 0, :], inc=("pe", 1))
        r.mm(B.pSC[:, 512:1024], B.onesf[:], B.rows[0:1, 1, :], inc=("pe", 1))
        r.wait("vector", "pe")
        r.cp(B.Bv[:, 0:512], B.pSC[:, 0:512])
        r.cp(B.Bv[:, 512:1024], B.pSC[:, 512:1024], inc=("dve", 1))
    ctx.phase(f"bv_a{L}_s{seq}", bv_flow)

    def ln1_flow(r):
        for g in range(NG):
            r.wait("gpsimd", "dx")
            r.wait("gpsimd", "pe")
            r.wait("gpsimd", "dve")
            r.dma("gpsimd", out=B.xb[:], in_=dram_grp(ctx.dram[src], seq, g), inc=("dw", 16))
            emit_ln(ctx, r, B, B.xb[:], B.xn[:, :, 1 + g * GRP: 1 + (g + 1) * GRP])
    ctx.phase(f"ln1_a{L}_s{seq}", ln1_flow)

    emit_kv_phase(ctx, B, seq, L)

    def attn_flow(r):
        for g in range(NG):
            gc = slice(1 + g * GRP, 1 + (g + 1) * GRP)
            tc = slice(g * GRP, (g + 1) * GRP)
            r.wait("sync", "dve")
            r.dma("sync", out=B.xres[:], in_=dram_grp(ctx.dram[src], seq, g), inc=("dx", 16))
            for m in range(NPAIR):
                slot = (g * NPAIR + m) % 4
                r.wait("gpsimd", "pe")
                r.dma("gpsimd", out=B.wsl[:, slot, :, :], in_=wslice_ap(ctx.dram[wq], m),
                      inc=("dw", 16))
                r.wait("tensor", "dw")
                r.wait("tensor", "dve", ctx.cnt["dve"])
                for k in range(NCH):
                    r.mm(B.pQ[:], B.wsl[:, slot, k, :], B.xn[:, k, gc],
                         start=(k == 0), stop=(k == NCH - 1),
                         inc=("pe", 1) if k == NCH - 1 else None)
                r.wait("scalar", "pe")
                r.act(B.qp[:], B.pQ[:], AF.Identity, bias=bqc[:, m:m + 1], scale=1.0,
                      inc=("act", 1))
                if L == 1:
                    r.wait("tensor", "act")
                    r.mm(B.pM[:], B.Pswp[:], B.qp[:], inc=("pe", 1))
                    r.wait("vector", "act")
                    r.tt(B.qtmp[:], B.qp[:], B.cosT[:, tc], op=ALU.mult)
                    r.wait("vector", "pe")
                    r.tt(B.qp[:], B.pM[:], B.sinT[:, tc], op=ALU.mult)
                    r.tt(B.qp[:], B.qp[:], B.qtmp[:], op=ALU.add, inc=("dve", 1))
                    r.wait("tensor", "dve")
                else:
                    r.wait("tensor", "act")
                pe0 = ctx.cnt["pe"]
                act0 = ctx.cnt["act"]
                for j in range(NJC):
                    psc = [B.pSC, B.pSC2][j % 2]
                    r.wait("tensor", "act", act0 + (j - 1))
                    r.mm(psc[:, 0:512], B.kT[0:64, m, j * 128:(j + 1) * 128], B.qp[0:64, :],
                         tile_position=(0, 0))
                    r.mm(psc[:, 512:1024], B.kT[64:128, m, j * 128:(j + 1) * 128],
                         B.qp[64:128, :], tile_position=(64, 0), inc=("pe", 1))
                    r.wait("scalar", "pe", pe0 + 2 * j + 1)
                    if j >= 4:
                        r.wait("scalar", "pe", pe0 + 2 * (j - 4) + 2)
                    r.act(B.expb[:, j % 4, :], psc[:], AF.Exp, bias=ccol(B, "eshift"),
                          scale=SCALE, inc=("act", 1))
                    r.wait("tensor", "act", act0 + j + 1)
                    r.mm(B.pavA[0:65, :], B.v_s[:, j, (2 * m) * 65:(2 * m) * 65 + 65],
                         B.expb[:, j % 4, 0:512], start=(j == 0), stop=(j == NJC - 1))
                    r.mm(B.pavB[0:65, :], B.v_s[:, j, (2 * m + 1) * 65:(2 * m + 1) * 65 + 65],
                         B.expb[:, j % 4, 512:1024], start=(j == 0), stop=(j == NJC - 1),
                         inc=("pe", 1))
                r.wait("vector", "pe")
                r.cp(B.avu[0:64, 0, :], B.pavA[0:64, :])
                r.cp(B.dsb[64:65, 0, :], B.pavA[64:65, :])
                r.cp(B.avu[0:64, 1, :], B.pavB[0:64, :])
                r.cp(B.dsb[64:65, 1, :], B.pavB[64:65, :])
                r.recip(B.dsb[64:65, 0, :], B.dsb[64:65, 0, :])
                r.recip(B.dsb[64:65, 1, :], B.dsb[64:65, 1, :], inc=("dve", 1))
                r.wait("tensor", "dve")
                r.mm(B.pQ[0:64, :], B.selb[:], B.dsb[:, 0, :], inc=("pe", 1))
                r.mm(B.pM[0:64, :], B.selb[:], B.dsb[:, 1, :], inc=("pe", 1))
                r.wait("vector", "pe")
                r.tt(B.ao[0:64, m, :], B.avu[0:64, 0, :], B.pQ[0:64, :], op=ALU.mult)
                r.tt(B.qtmp[0:64, :], B.avu[0:64, 1, :], B.pM[0:64, :],
                     op=ALU.mult, inc=("dve", 1))
                r.wait("tensor", "dve")
                r.mm(B.pavA[64:128, :], B.I64[:], B.qtmp[0:64, :], tile_position=(0, 64),
                     inc=("pe", 1))
                r.wait("vector", "pe")
                r.cp(B.ao[64:128, m, :], B.pavA[64:128, :], inc=("dve", 1))
            for m2 in range(NCH):
                slot = m2 % 4
                r.wait("gpsimd", "pe")
                r.dma("gpsimd", out=B.wsl[:, slot, :, :], in_=wslice_ap(ctx.dram[wo], m2),
                      inc=("dw", 16))
                ps = [B.pSC, B.pSC2][m2 % 2][:, 0:512]
                r.wait("tensor", "dw")
                r.wait("tensor", "act", ctx.cnt["act"])
                r.wait("tensor", "dve", ctx.cnt["dve"])
                for k in range(NCH):
                    r.mm(ps, B.wsl[:, slot, k, :], B.ao[:, k, :], start=(k == 0),
                         stop=(k == NCH - 1), inc=("pe", 1) if k == NCH - 1 else None)
                r.wait("scalar", "pe")
                r.act(B.yb[:, m2, :], ps, AF.Identity, bias=boc[:, m2:m2 + 1], scale=1.0,
                      inc=("act", 1))
            r.wait("vector", "act")
            emit_ln(ctx, r, B, B.yb[:], None, resid=True,
                    store=dram_grp(ctx.dram[dst], seq, g))
    ctx.phase(f"attn_a{L}_s{seq}", attn_flow)


def emit_ffn0(ctx, B, seq, src, dst):
    def flow(r):
        for g in range(NG):
            r.wait("gpsimd", "dx")
            r.wait("gpsimd", "pe")
            r.wait("gpsimd", "dve")
            r.dma("gpsimd", out=B.xb[:], in_=dram_grp(ctx.dram[src], seq, g), inc=("dw", 16))
            emit_ln(ctx, r, B, B.xb[:], B.xn[:, :, 1 + g * GRP:1 + (g + 1) * GRP])
            r.wait("sync", "dve")
            r.dma("sync", out=B.xres[:], in_=dram_grp(ctx.dram[src], seq, g), inc=("dx", 16))
            for m2 in range(NCH):
                slot = m2 % 4
                r.wait("sync", "act")
                r.dma("sync", out=B.wsl[:, slot, :, :], in_=wslice_ap(ctx.dram["tdram"], m2),
                      inc=("dx", 16))
                ps = [B.pSC, B.pSC2][m2 % 2][:, 0:512]
                r.wait("tensor", "dx")
                r.wait("tensor", "act", ctx.cnt["act"])
                r.wait("tensor", "dve", ctx.cnt["dve"])
                for k in range(NCH):
                    r.mm(ps, B.wsl[:, slot, k, :], B.xn[:, k, 1 + g * GRP:1 + (g + 1) * GRP],
                         start=(k == 0), stop=(k == NCH - 1),
                         inc=("pe", 1) if k == NCH - 1 else None)
                r.wait("scalar", "pe")
                r.act(B.yb[:, m2, :], ps, AF.Identity, bias=ccol(B, "fb", 8)[:, m2:m2 + 1],
                      scale=ccol(B, "alpha"), inc=("act", 1))
            r.wait("vector", "act")
            emit_ln(ctx, r, B, B.yb[:], None, resid=True,
                    store=dram_grp(ctx.dram[dst], seq, g))
    ctx.phase(f"ffn0_s{seq}", flow)


def emit_ffn1(ctx, B, seq, src, dst):
    def flow(r):
        for g in range(NG):
            gc = slice(1 + g * GRP, 1 + (g + 1) * GRP)
            pc = slice(g * GRP, g * GRP + GRP)
            r.wait("gpsimd", "dx")
            r.wait("gpsimd", "pe")
            r.wait("gpsimd", "dve")
            r.dma("gpsimd", out=B.xb[:], in_=dram_grp(ctx.dram[src], seq, g), inc=("dw", 16))
            emit_ln(ctx, r, B, B.xb[:], B.xn[:, :, gc])
            r.wait("sync", "dve")
            r.dma("sync", out=B.xres[:], in_=dram_grp(ctx.dram[src], seq, g), inc=("dx", 16))
            r.wait("vector", "dve", ctx.cnt["dve"])
            r.wait("vector", "pe", ctx.cnt["pe"])
            for c in range(NCH):
                r.ts(B.t1[:, c, :], B.xn[:, c, gc], ccol(B, "mur", 8)[:, c:c + 1], None,
                     op0=ALU.mult)
                r.ts(B.qtmp[:], B.xn[:, c, pc], ccol(B, "murc", 8)[:, c:c + 1], None,
                     op0=ALU.mult)
                r.tt(B.t1[:, c, :], B.t1[:, c, :], B.qtmp[:], op=ALU.add)
                r.ts(B.t2[:, c, :], B.xn[:, c, gc], ccol(B, "muk", 8)[:, c:c + 1], None,
                     op0=ALU.mult)
                r.ts(B.qtmp[:], B.xn[:, c, pc], ccol(B, "mukc", 8)[:, c:c + 1], None,
                     op0=ALU.mult)
                r.tt(B.t2[:, c, :], B.t2[:, c, :], B.qtmp[:], op=ALU.add,
                     inc=("dve", 1) if c == NCH - 1 else None)
            for m2 in range(NCH):
                slot = m2 % 4
                r.wait("gpsimd", "act")
                r.dma("gpsimd", out=B.wsl[:, slot, :, :], in_=wslice_ap(ctx.dram["wr1t"], m2),
                      inc=("dw", 16))
                ps = [B.pSC, B.pSC2][m2 % 2][:, 0:512]
                r.wait("tensor", "dw")
                r.wait("tensor", "dve", ctx.cnt["dve"])
                r.wait("tensor", "act", ctx.cnt["act"])
                for k in range(NCH):
                    r.mm(ps, B.wsl[:, slot, k, :], B.t1[:, k, :], start=(k == 0),
                         stop=(k == NCH - 1), inc=("pe", 1) if k == NCH - 1 else None)
                r.wait("scalar", "pe")
                r.act(B.fr[:, m2, :], ps, AF.Sigmoid, bias=ccol(B, "zero"), scale=1.0,
                      inc=("act", 1))
            for m2 in range(NCH):
                slot = m2 % 4
                r.wait("gpsimd", "act")
                r.wait("gpsimd", "dve")
                r.dma("gpsimd", out=B.wsl[:, slot, :, :], in_=wslice_ap(ctx.dram["wk2t"], m2),
                      inc=("dw", 16))
                ps = [B.pSC, B.pSC2][m2 % 2][:, 0:512]
                r.wait("tensor", "dw")
                r.wait("tensor", "dve", ctx.cnt["dve"])
                r.wait("tensor", "act", ctx.cnt["act"])
                for k in range(NCH):
                    r.mm(ps, B.wsl[:, slot, k, :], B.t2[:, k, :], start=(k == 0),
                         stop=(k == NCH - 1), inc=("pe", 1) if k == NCH - 1 else None)
                r.wait("vector", "pe")
                r.ts(B.qtmp[:], ps, 0.0, None, op0=ALU.max)
                r.tt(B.fk[:, m2, :], B.qtmp[:], B.qtmp[:], op=ALU.mult, inc=("dve", 1))
            for m2 in range(NCH):
                slot = m2 % 4
                r.wait("gpsimd", "dve")
                r.dma("gpsimd", out=B.wsl[:, slot, :, :], in_=wslice_ap(ctx.dram["wv2t"], m2),
                      inc=("dw", 16))
                ps = [B.pSC, B.pSC2][m2 % 2][:, 0:512]
                r.wait("tensor", "dw")
                r.wait("tensor", "dve", ctx.cnt["dve"])
                for k in range(NCH):
                    r.mm(ps, B.wsl[:, slot, k, :], B.fk[:, k, :], start=(k == 0),
                         stop=(k == NCH - 1), inc=("pe", 1) if k == NCH - 1 else None)
                r.wait("vector", "pe")
                r.tt(B.yb[:, m2, :], B.fr[:, m2, :], ps, op=ALU.mult, inc=("dve", 1))
            r.wait("vector", "dve", ctx.cnt["dve"])
            emit_ln(ctx, r, B, B.yb[:], None, resid=True,
                    store=dram_grp(ctx.dram[dst], seq, g))
    ctx.phase(f"ffn1_s{seq}", flow)


def emit_prep_consts(ctx, B, need):
    d = ctx.dram

    def flow(r):
        r.ms(B.onesb[:], 1.0)
        r.ms(B.onesf[:], 1.0)
        r.ms(B.negf[:], -1.0)
        r.ms(B.onesD[:], 1.0 / 1024.0)
        r.ms(B.onesP[:], 1.0)
        r.ms(B.e11[:], EPS)
        r.ms(B.z11[:], 0.0)
        r.ms(B.cols[:], 0.0)
        r.ms(ccol(B, "eshift"), -ESHIFT)
        r.ms(B.xn[:, :, 0:1], 0.0)
        r.ms(B.selb[:], 0.0)
        r.ms(B.selb[64:65, :], 1.0)
        vv = B.v_s[:].rearrange("p j (h c) -> p j h c", c=65)
        r.ms(vv[:, :, :, 64:65], 1.0, inc=("pool", 1))
        r.wait("sync", "pool")
        loads = []
        if SB_A0 in need:
            loads += [("bq0pc", "bq0", 8), ("bk0pc", "bk0", 8), ("bo0pc", "bo0", 8),
                      ("bl0pc", "bl0", 2)]
        if SB_F0 in need:
            loads += [("fb0pc", "fb", 8)]
        if SB_A1 in need:
            loads += [("bq1pc", "bq1", 8), ("bk1pc", "bk1", 8), ("bo1pc", "bo1", 8)]
        if SB_F1 in need:
            loads += [("murpc", "mur", 8), ("mukpc", "muk", 8)]
        for nm, cn, w in loads:
            r.dma("sync", out=ccol(B, cn, w), in_=d[nm], inc=("dx", 16))
        if SB_F0 in need:
            r.dma("sync", out=B.rows[0:1, 0, 0:1], in_=d["alpha"], inc=("dx", 16))
            r.wait("tensor", "dx")
            r.mm(B.pQ[:, 0:1], B.onesf[:], B.rows[0:1, 0, 0:1], inc=("pe", 1))
            r.wait("vector", "pe")
            r.cp(ccol(B, "alpha"), B.pQ[:, 0:1], inc=("dve", 1))
        if SB_F1 in need:
            r.wait("vector", "dx")
            r.ts(ccol(B, "murc", 8), ccol(B, "mur", 8), -1.0, 1.0, op0=ALU.mult, op1=ALU.add)
            r.ts(ccol(B, "mukc", 8), ccol(B, "muk", 8), -1.0, 1.0, op0=ALU.mult, op1=ALU.add,
                 inc=("dve", 1))
        if SB_A1 in need:
            r.iota(B.scrI[:, 0:1], [[1, 1]], cm=1, inc=("pool", 1))
            r.wait("vector", "pool")
            r.ts(B.scrI[:, 1:2], B.scrI[:, 0:1], 32, None, op0=ALU.bitwise_xor)
            r.cp(ccol(B, "pf"), B.scrI[:, 0:1])
            r.cp(ccol(B, "pxorf"), B.scrI[:, 1:2], inc=("dve", 1))
            r.iota(B.rows[0:1, 1, 0:128], [[1, 128]], cm=0, inc=("pool", 1))
            r.wait("tensor", "pool")
            r.wait("tensor", "dve", ctx.cnt["dve"])
            r.mm(B.pM[:, 0:128], B.onesf[:], B.rows[0:1, 1, 0:128], inc=("pe", 1))
            r.wait("vector", "pe")
            r.ts(B.scrA[:, 0:128], B.pM[:, 0:128], ccol(B, "pxorf"), None, op0=ALU.subtract)
            r.tt(B.scrA[:, 0:128], B.scrA[:, 0:128], B.scrA[:, 0:128], op=ALU.mult)
            r.ts(B.scrA[:, 0:128], B.scrA[:, 0:128], 1.0, None, op0=ALU.min)
            r.ts(B.Pswp[:], B.scrA[:, 0:128], -1.0, 1.0, op0=ALU.mult, op1=ALU.add)
            r.ts(B.scrA[:, 0:128], B.pM[:, 0:128], ccol(B, "pf"), None, op0=ALU.subtract)
            r.tt(B.scrA[:, 0:128], B.scrA[:, 0:128], B.scrA[:, 0:128], op=ALU.mult)
            r.ts(B.scrA[:, 0:128], B.scrA[:, 0:128], 1.0, None, op0=ALU.min)
            r.ts(B.scrB[:, 0:128], B.scrA[:, 0:128], -1.0, 1.0, op0=ALU.mult, op1=ALU.add)
            r.cp(B.I64[:], B.scrB[0:64, 0:64], inc=("dve", 1))
        if SB_A0 in need and SB_A1 not in need:
            r.iota(B.scrI[:, 0:1], [[1, 1]], cm=1, inc=("pool", 1))
            r.wait("vector", "pool")
            r.cp(ccol(B, "pf"), B.scrI[:, 0:1], inc=("dve", 1))
            r.iota(B.rows[0:1, 1, 0:128], [[1, 128]], cm=0, inc=("pool", 1))
            r.wait("tensor", "pool")
            r.wait("tensor", "dve", ctx.cnt["dve"])
            r.mm(B.pM[:, 0:128], B.onesf[:], B.rows[0:1, 1, 0:128], inc=("pe", 1))
            r.wait("vector", "pe")
            r.ts(B.scrA[:, 0:128], B.pM[:, 0:128], ccol(B, "pf"), None, op0=ALU.subtract)
            r.tt(B.scrA[:, 0:128], B.scrA[:, 0:128], B.scrA[:, 0:128], op=ALU.mult)
            r.ts(B.scrA[:, 0:128], B.scrA[:, 0:128], 1.0, None, op0=ALU.min)
            r.ts(B.scrB[:, 0:128], B.scrA[:, 0:128], -1.0, 1.0, op0=ALU.mult, op1=ALU.add)
            r.cp(B.I64[:], B.scrB[0:64, 0:64], inc=("dve", 1))
    ctx.phase("prep_consts", flow)


def emit_prep_tables(ctx, B, tag=""):
    LN1E4_32 = float(np.log(10000.0) / 32.0)
    INV2PI = float(1.0 / (2 * np.pi))
    TWOPI = float(2 * np.pi)

    def flow1(r):
        r.wait("vector", "dx")
        r.wait("vector", "dw")
        r.wait("gpsimd", "dx")
        r.iota(B.scrI[:, 0:1], [[1, 1]], cm=1, inc=("pool", 1))
        r.wait("vector", "pool")
        r.ts(B.scrI[:, 2:3], B.scrI[:, 0:1], 32, None, op0=ALU.bitwise_and)
        r.ts(B.scrI[:, 1:2], B.scrI[:, 0:1], 31, None, op0=ALU.bitwise_and)
        r.cp(B.scrB[:, 0:1], B.scrI[:, 2:3])
        r.cp(B.scrA[:, 0:1], B.scrI[:, 1:2], inc=("dve", 1))
        r.wait("scalar", "dve")
        r.act(ccol(B, "absacc"), B.scrA[:, 0:1], AF.Exp, bias=ccol(B, "zero"), scale=-LN1E4_32,
              inc=("act", 1))
        r.wait("vector", "act")
        r.ts(ccol(B, "pxorf"), B.scrB[:, 0:1], 1.0 / 16.0, -1.0, op0=ALU.mult, op1=ALU.add,
             inc=("dve", 1))
    ctx.phase(f"prep_tab1{tag}", flow1)

    def flow2b(r):
        r.iota(B.rows[0:1, 1, :], [[1, GRP]], cm=0, inc=("pool", 1))
        for cc in range(4):
            base = float(cc * GRP)
            sl = slice(cc * GRP, (cc + 1) * GRP)
            r.wait("tensor", "pool")
            r.wait("tensor", "dve", ctx.cnt["dve"])
            r.mm(B.pQ[:], B.onesf[:], B.rows[0:1, 1, :], inc=("pe", 1))
            r.wait("vector", "pe")
            r.ts(B.scrC[:], B.pQ[:], base, None, op0=ALU.add)
            r.ts(B.scrC[:], B.scrC[:], ccol(B, "absacc"), None, op0=ALU.mult)
            r.ts(B.scrB[:], B.scrC[:], INV2PI, None, op0=ALU.mult)
            r.cp(B.scrI[:], B.scrB[:])
            r.cp(B.scrA[:], B.scrI[:])
            r.tt(B.scrB[:], B.scrB[:], B.scrA[:], op=ALU.subtract, inc=("dve", 1))
            r.wait("scalar", "dve")
            r.act(B.scrB[:], B.scrB[:], AF.Sin, bias=ccol(B, "zero"), scale=TWOPI, inc=("act", 1))
            r.wait("vector", "act")
            r.ts(B.sinT[:, sl], B.scrB[:], ccol(B, "pxorf"), None, op0=ALU.mult)
            r.ts(B.scrB[:], B.scrC[:], INV2PI, 0.25, op0=ALU.mult, op1=ALU.add)
            r.cp(B.scrI[:], B.scrB[:])
            r.cp(B.scrA[:], B.scrI[:])
            r.tt(B.scrB[:], B.scrB[:], B.scrA[:], op=ALU.subtract, inc=("dve", 1))
            r.wait("scalar", "dve")
            r.act(B.scrB[:], B.scrB[:], AF.Sin, bias=ccol(B, "zero"), scale=TWOPI, inc=("act", 1))
            r.wait("vector", "act")
            r.cp(B.cosT[:, sl], B.scrB[:], inc=("dve", 1))
    ctx.phase(f"prep_tab2{tag}", flow2b)


def emit_prep_ternary(ctx, B):
    d = ctx.dram
    wr = d["fw0t"].rearrange("(c p) o -> p c o", p=128)
    tr = d["tdram"].rearrange("(c p) o -> p c o", p=128)

    def flow(r):
        for c in range(NCH):
            r.wait("sync", "dve", ctx.cnt["dve"])
            r.dma("sync", out=B.scrW[:], in_=wr[:, c, :], inc=("dx", 16))
            r.wait("vector", "dx")
            r.red(ccol(B, "absacc", 8)[:, c:c + 1], B.scrW[:], ALU.add, absval=True,
                  inc=("dve", 1))
        r.cp(B.scrB[:], B.scrW[:, 0:512])
        r.red(B.dcol[:], ccol(B, "absacc", 8), ALU.add)
        r.cp(B.scrB[:], B.scrW[:, 0:512])
        r.tt(B.scrB[:], B.scrB[:], B.scrB[:], op=ALU.mult, inc=("dve", 1))
        r.wait("tensor", "dve")
        r.mm(B.pQ[0:1, 0:1], B.dcol[:], B.onesP[:], inc=("pe", 1))
        r.wait("vector", "pe")
        r.ts(B.rows[0:1, 0, 0:1], B.pQ[0:1, 0:1], 0.7 / (1024.0 * 1024.0), None,
             op0=ALU.mult, inc=("dve", 1))
        r.wait("tensor", "dve")
        r.mm(B.pQ[:, 0:1], B.onesf[:], B.rows[0:1, 0, 0:1], inc=("pe", 1))
        r.wait("vector", "pe")
        r.cp(ccol(B, "delta"), B.pQ[:, 0:1], inc=("dve", 1))
        for c in range(NCH):
            r.wait("sync", "dx", ctx.cnt["dx"])
            r.wait("sync", "dve", ctx.cnt["dve"])
            r.wait("sync", "dw", ctx.cnt["dw"])
            r.dma("sync", out=B.scrW[:], in_=wr[:, c, :], inc=("dx", 16))
            r.wait("scalar", "dx")
            r.act(B.rtmp[:, 0:1024], B.scrW[:], AF.Sign, bias=ccol(B, "zero"), scale=1.0)
            r.act(B.scrW2[:], B.scrW[:], AF.Abs, bias=ccol(B, "zero"), scale=1.0,
                  inc=("act", 1))
            r.wait("vector", "act")
            r.ts(B.scrW[:], B.scrW2[:], ccol(B, "delta"), 0.0, op0=ALU.subtract,
                 op1=ALU.max, inc=("dve", 1))
            r.wait("scalar", "dve")
            r.act(B.scrW2[:], B.scrW[:], AF.Sign, bias=ccol(B, "zero"), scale=1.0,
                  inc=("act", 1))
            r.wait("vector", "act")
            r.tt(B.scrW[:], B.rtmp[:, 0:1024], B.scrW2[:], op=ALU.mult, inc=("dve", 1))
            r.wait("gpsimd", "dve")
            r.dma("gpsimd", out=tr[:, c, :], in_=B.scrW[:], inc=("dw", 16))
        r.wait("sync", "dw")
    ctx.phase("prep_tern", flow)


WEIGHT_SPECS = {
    "Wq0": [D, D], "Wl0": [D, LAT], "Wk0": [LAT, D], "Wv0": [LAT, D], "Wo0": [D, D],
    "fw0t": [D, D],
    "wq1t": [D, D], "wk1t": [D, D], "wv1t": [D, D], "wo1t": [D, D],
    "wr1t": [D, D], "wk2t": [D, D], "wv2t": [D, D],
}
PCOL_SPECS = {
    "bq0pc": [128, 8], "bk0pc": [128, 8], "bo0pc": [128, 8], "bl0pc": [128, 2],
    "fb0pc": [128, 8], "bq1pc": [128, 8], "bk1pc": [128, 8], "bo1pc": [128, 8],
    "murpc": [128, 8], "mukpc": [128, 8],
}
ROW_SPECS = {"bv0row": [1, D], "bv1row": [1, D], "alpha": [1, 1]}

SB_WEIGHTS = {
    SB_A0: ["Wq0", "Wl0", "Wk0", "Wv0", "Wo0", "bq0pc", "bk0pc", "bo0pc", "bl0pc", "bv0row"],
    SB_F0: ["fw0t", "fb0pc", "alpha"],
    SB_A1: ["wq1t", "wk1t", "wv1t", "wo1t", "bq1pc", "bk1pc", "bo1pc", "bv1row"],
    SB_F1: ["wr1t", "wk2t", "wv2t", "murpc", "mukpc"],
}


def build_program(parts):
    nc = bass.Bass("TRN2", target_bir_lowering=False)
    dram = {}
    dram["x"] = nc.dram_tensor("x", [D, T], F32, kind="ExternalInput").ap()
    dram["y"] = nc.dram_tensor("y", [D, T], F32, kind="ExternalOutput").ap()
    if len(parts) > 1:
        dram["xbuf"] = nc.dram_tensor("xbuf", [D, T], F32, kind="Internal").ap()
    if SB_F0 in parts:
        dram["tdram"] = nc.dram_tensor("tdram", [D, D], BF16, kind="Internal").ap()
    names = set()
    for p in parts:
        names.update(SB_WEIGHTS[p])
    for n in sorted(names):
        if n in WEIGHT_SPECS:
            dram[n] = nc.dram_tensor(n, WEIGHT_SPECS[n], F32, kind="ExternalInput").ap()
        elif n in PCOL_SPECS:
            dram[n] = nc.dram_tensor(n, PCOL_SPECS[n], F32, kind="ExternalInput").ap()
        else:
            dram[n] = nc.dram_tensor(n, ROW_SPECS[n], F32, kind="ExternalInput").ap()

    with ExitStack() as stack:
        E = stack.enter_context
        B = alloc_buffers(nc, stack)
        sems = {}
        for s in ["dx", "dw", "pe", "act", "dve", "pool"]:
            sems[s] = E(nc.semaphore(s))
        ctx = Ctx(nc, sems)
        ctx.dram = dram

        emit_prep_consts(ctx, B, set(parts))
        if SB_F0 in parts:
            emit_prep_ternary(ctx, B)

        for i, p in enumerate(parts):
            src = "x" if i == 0 else "xbuf"
            dst = "y" if i == len(parts) - 1 else "xbuf"
            for seq in range(NSEQ):
                if p == SB_A0:
                    emit_attn(ctx, B, seq, 0, src, dst)
                elif p == SB_F0:
                    emit_ffn0(ctx, B, seq, src, dst)
                elif p == SB_A1:
                    emit_attn(ctx, B, seq, 1, src, dst)
                elif p == SB_F1:
                    emit_ffn1(ctx, B, seq, src, dst)

        def fin(r):
            r.wait("sync", "dx")
            r.wait("sync", "dw")
        ctx.phase("fin", fin)
    return nc


def _pcol(b):
    return np.ascontiguousarray(np.asarray(b, np.float32).reshape(-1, 128).T)


def _prep_shared(inputs):
    f = lambda a: np.ascontiguousarray(np.asarray(a, np.float32))
    tr = lambda a: np.ascontiguousarray(np.asarray(a, np.float32).T)
    sh = {
        "Wq0": f(inputs["L0_attn_Wq"]), "Wl0": f(inputs["L0_attn_Wl"]),
        "Wk0": f(inputs["L0_attn_Wk"]), "Wv0": f(inputs["L0_attn_Wv"]),
        "Wo0": f(inputs["L0_attn_Wo"]),
        "fw0t": tr(inputs["L0_ffn_weight"]),
        "wq1t": tr(inputs["L1_attn_wq"]), "wk1t": tr(inputs["L1_attn_wk"]),
        "wv1t": tr(inputs["L1_attn_wv"]), "wo1t": tr(inputs["L1_attn_wo"]),
        "wr1t": tr(inputs["L1_ffn_W_r"]), "wk2t": tr(inputs["L1_ffn_W_k"]),
        "wv2t": tr(inputs["L1_ffn_W_v"]),
        "bq0pc": _pcol(inputs["L0_attn_bq"]), "bk0pc": _pcol(inputs["L0_attn_bk"]),
        "bo0pc": _pcol(inputs["L0_attn_bo"]), "bl0pc": _pcol(inputs["L0_attn_bl"]),
        "fb0pc": _pcol(inputs["L0_ffn_bias"]),
        "bq1pc": _pcol(inputs["L1_attn_bq"]), "bk1pc": _pcol(inputs["L1_attn_bk"]),
        "bo1pc": _pcol(inputs["L1_attn_bo"]),
        "murpc": _pcol(inputs["L1_ffn_mu_r"]), "mukpc": _pcol(inputs["L1_ffn_mu_k"]),
        "bv0row": f(inputs["L0_attn_bv"]).reshape(1, D),
        "bv1row": f(inputs["L1_attn_bv"]).reshape(1, D),
        "alpha": f(inputs["L0_ffn_alpha"]).reshape(1, 1),
    }
    return sh


def _launch(nc, in_maps):
    res = run_bass_kernel_spmd(nc, in_maps, core_ids=list(range(8)))
    return [r["y"] for r in res.results]


def kernel(**inputs):
    x = np.asarray(inputs["L0_attn_x"], np.float32)
    sh = _prep_shared(inputs)
    xcores = []
    for c in range(8):
        xs = x[c * NSEQ:(c + 1) * NSEQ]
        xcores.append(np.ascontiguousarray(xs.transpose(2, 0, 1).reshape(D, T)))

    if MODE == "fused":
        nc = build_program([SB_A0, SB_F0, SB_A1, SB_F1])
        in_maps = []
        for c in range(8):
            m = dict(sh)
            m["x"] = xcores[c]
            in_maps.append(m)
        ys = _launch(nc, in_maps)
    else:
        cur = xcores
        for p in (SB_A0, SB_F0, SB_A1, SB_F1):
            nc = build_program([p])
            in_maps = []
            for c in range(8):
                m = {k: sh[k] for k in SB_WEIGHTS[p]}
                m["x"] = cur[c]
                in_maps.append(m)
            cur = _launch(nc, in_maps)
        ys = cur

    out = np.stack([np.asarray(yc, np.float32).reshape(D, NSEQ, S).transpose(1, 2, 0)
                    for yc in ys])
    return np.ascontiguousarray(out.reshape(16, S, D))


# version 02
# speedup vs baseline: 1.0061x; 1.0061x over previous
import numpy as np
from contextlib import ExitStack
import concourse.bass as bass
import concourse.mybir as mybir
from concourse.bass_utils import run_bass_kernel_spmd
from concourse.bass import broadcast_tensor_aps

F32 = mybir.dt.float32
BF16 = mybir.dt.bfloat16
I32 = mybir.dt.int32
AF = mybir.ActivationFunctionType
ALU = mybir.AluOpType
AX = mybir.AxisListType

D = 1024
S = 2048
NSEQ = 2
T = NSEQ * S
NCH = 8
GRP = 512
NG = S // GRP
H = 16
DH = 64
NPAIR = 8
LAT = 256
SCALE = 0.125
ESHIFT = 16.0
EPS = 1e-5
NJC = S // 128

SB_A0, SB_F0, SB_A1, SB_F1 = 0, 1, 2, 3

MODE = "fused"


ROLE_ATTRS = [("sync", "sync"), ("tensor", "tensor"), ("vector", "vector"),
              ("scalar", "scalar"), ("gpsimd", "gpsimd")]


class Ctx:
    def __init__(self, nc, sems):
        self.nc = nc
        self.sem = sems
        self.cnt = {k: 0 for k in sems}

    def phase(self, name, flow):
        base = dict(self.cnt)

        def fenced(r):
            for s in ("pe", "act", "dve", "pool", "dx", "dw"):
                r.wait(r.role, s)
            flow(r)

        with self.nc.Block(name) as blk:
            for role, attr in ROLE_ATTRS:
                def mk(role):
                    def body(eng):
                        self.cnt = dict(base)
                        fenced(R(self, role, eng))
                    return body
                getattr(blk, attr)(mk(role))


class R:
    __slots__ = ("ctx", "role", "eng")

    def __init__(self, ctx, role, eng):
        self.ctx = ctx
        self.role = role
        self.eng = eng

    def _fin(self, inst, inc):
        if inc is not None:
            name, n = inc
            if inst is not None:
                inst.then_inc(self.ctx.sem[name], n)
            self.ctx.cnt[name] += n

    def wait(self, role, sem, val=None):
        v = self.ctx.cnt[sem] if val is None else val
        if v <= 0:
            return
        if self.role == role:
            self.eng.wait_ge(self.ctx.sem[sem], v)

    def mm(self, out, lhsT, rhs, start=True, stop=True, tile_position=None, inc=None):
        inst = None
        if self.role == "tensor":
            inst = self.eng.matmul(out, lhsT, rhs, start=start, stop=stop,
                                   tile_position=tile_position)
        self._fin(inst, inc)

    def act(self, out, in_, func, bias, scale=1.0, inc=None):
        inst = None
        if self.role == "scalar":
            inst = self.eng.activation(out, in_, func, bias=bias, scale=scale)
        self._fin(inst, inc)

    def tt(self, out, in0, in1, op, inc=None):
        inst = None
        if self.role == "vector":
            inst = self.eng.tensor_tensor(out, in0, in1, op=op)
        self._fin(inst, inc)

    def ttb(self, out, in0, in1b, op, inc=None):
        inst = None
        if self.role == "vector":
            b = broadcast_tensor_aps(in1b, in0)[0]
            inst = self.eng.tensor_tensor(out, in0, b, op=op)
        self._fin(inst, inc)

    def ts(self, out, in0, s1, s2, op0, op1=None, inc=None):
        inst = None
        if self.role == "vector":
            if op1 is None:
                inst = self.eng.tensor_scalar(out, in0, s1, None, op0=op0)
            else:
                inst = self.eng.tensor_scalar(out, in0, s1, s2, op0=op0, op1=op1)
        self._fin(inst, inc)

    def cp(self, out, in_, inc=None):
        inst = None
        if self.role == "vector":
            inst = self.eng.tensor_copy(out, in_)
        self._fin(inst, inc)

    def red(self, out, in_, op, absval=False, inc=None):
        inst = None
        if self.role == "vector":
            inst = self.eng.tensor_reduce(out, in_, AX.X, op,
                                          apply_absolute_value=absval)
        self._fin(inst, inc)

    def recip(self, out, in_, inc=None):
        inst = None
        if self.role == "vector":
            inst = self.eng.reciprocal(out, in_)
        self._fin(inst, inc)

    def ms(self, ap, val, inc=None):
        inst = None
        if self.role == "gpsimd":
            inst = self.eng.memset(ap, val)
        self._fin(inst, inc)

    def iota(self, ap, pattern, cm, inc=None):
        inst = None
        if self.role == "gpsimd":
            inst = self.eng.iota(ap, pattern, channel_multiplier=cm,
                                 allow_small_or_imprecise_dtypes=True)
        self._fin(inst, inc)

    def pow_tt(self, out, in0, in1, inc=None):
        inst = None
        if self.role == "gpsimd":
            inst = self.eng.tensor_tensor(out, in0, in1, op=ALU.pow)
        self._fin(inst, inc)

    def dma(self, role, out, in_, inc=None):
        inst = None
        if self.role == role:
            inst = self.eng.dma_start(out=out, in_=in_)
        self._fin(inst, inc)


class NS:
    pass


def alloc_buffers(nc, stack):
    E = stack.enter_context
    B = NS()
    sb = nc.sbuf_tensor
    B.xb = E(sb("xb", [128, NCH, GRP], BF16))
    B.xres = E(sb("xres", [128, NCH, GRP], F32))
    B.xn = E(sb("xn", [128, NCH, S + 1], BF16))
    B.kT = E(sb("kT", [128, NPAIR, S], BF16))
    B.v_s = E(sb("v_s", [128, NJC, H * 65], BF16))
    B.ctab = E(sb("ctab", [128, 2, S], BF16))
    B.qp = E(sb("qp", [128, GRP], BF16))
    B.qtmp = E(sb("qtmp", [128, GRP], BF16))
    B.rtmp = E(sb("rtmp", [128, S], BF16))
    B.expb = E(sb("expb", [128, 4, 1024], BF16))
    B.avu = E(sb("avu", [128, 2, GRP], BF16))
    B.dsb = E(sb("dsb", [65, 2, GRP], F32))
    B.selb = E(sb("selb", [65, 64], F32))
    B.ao = E(sb("ao", [128, NPAIR, GRP], BF16))
    B.yb = E(sb("yb", [128, NCH, GRP], BF16))
    B.t1 = E(sb("t1", [128, NCH, GRP], BF16))
    B.t2 = E(sb("t2", [128, NCH, GRP], BF16))
    B.wsl = E(sb("wsl", [128, 4, NCH, 128], BF16))
    B.Pswp = E(sb("Pswp", [128, 128], BF16))
    B.I64 = E(sb("I64", [64, 64], BF16))
    B.Bv = E(sb("Bv", [128, D], BF16))
    B.rbs = E(sb("rbs", [128, GRP], BF16))
    B.abs_ = E(sb("abs_", [128, GRP], BF16))
    B.cols = E(sb("cols", [128, 128], F32))
    B.rows = E(sb("rows", [1, 3, GRP], F32))
    B.onesb = E(sb("onesb", [1, 128], BF16))
    B.onesf = E(sb("onesf", [1, 128], F32))
    B.negf = E(sb("negf", [1, 128], F32))
    B.onesD = E(sb("onesD", [128, 1], BF16))
    B.onesP = E(sb("onesP", [128, 1], F32))
    B.dcol = E(sb("dcol", [128, 1], F32))
    B.e11 = E(sb("e11", [1, 1], F32))
    B.z11 = E(sb("z11", [1, 1], F32))
    B.scrI = E(sb("scrI", [128, GRP], I32))
    B.latT = B.ctab
    B.cosT = B.ctab[:, 0, :]
    B.sinT = B.ctab[:, 1, :]
    B.fr = B.ao
    B.fk = B.expb[:].rearrange("p a b -> p (a b)").rearrange("p (c t) -> p c t", t=GRP)
    B.wrhs = B.t2
    B.scrA = B.xres[:, 0, :]
    B.scrB = B.xres[:, 1, :]
    B.scrC = B.xres[:, 2, :]
    B.scrW = B.xres[:, 4:6, :].rearrange("p c t -> p (c t)")
    B.scrW2 = B.xres[:, 6:8, :].rearrange("p c t -> p (c t)")
    B.brow = B.rows[0:1, 0:2, :].rearrange("a c t -> a (c t)")
    ps = nc.psum_tensor
    B.pSC = E(ps("pSC", [128, 1024], F32))
    B.pSC2 = E(ps("pSC2", [128, 1024], F32))
    B.pQ = E(ps("pQ", [128, 512], F32))
    B.pM = E(ps("pM", [128, 512], F32))
    B.pavA = E(ps("pavA", [128, 512], F32))
    B.pavB = E(ps("pavB", [128, 512], F32))
    B.col = {
        "bq0": 0, "bk0": 8, "bo0": 16, "fb": 24, "bq1": 32, "bk1": 40,
        "bo1": 48, "mur": 56, "muk": 64, "murc": 72, "mukc": 80,
        "bl0": 88,
        "alpha": 90, "eshift": 91, "zero": 92, "delta": 93, "absacc": 94,
        "pxorf": 102, "pf": 103,
    }
    return B


def ccol(B, name, w=1):
    o = B.col[name]
    return B.cols[:, o:o + w]


def ccol_off(name):
    return {"bq0": 0, "bk0": 8, "bo0": 16, "fb": 24, "bq1": 32, "bk1": 40,
            "bo1": 48, "mur": 56, "muk": 64, "murc": 72, "mukc": 80,
            "bl0": 88, "alpha": 90, "eshift": 91, "zero": 92, "delta": 93,
            "absacc": 94, "pxorf": 102, "pf": 103}[name]


def B_cols_ap(B):
    return B.cols[:]


def emit_ln(ctx, r, B, src3, dst3, resid=None, store=None, loadres=None):
    r.wait("vector", "dw")
    r.wait("vector", "pe")
    r.tt(B.t1[:], src3, src3, op=ALU.mult, inc=("dve", 1))
    if loadres is not None:
        r.wait("sync", "dve")
        r.dma("sync", out=B.xres[:], in_=loadres, inc=("dx", 16))
    r.wait("tensor", "dw")
    for c in range(NCH):
        r.mm(B.pQ[0:1, :], B.onesD[:], src3[:, c, :], start=(c == 0), stop=(c == NCH - 1),
             inc=("pe", 1) if c == NCH - 1 else None)
    r.wait("tensor", "dve")
    for c in range(NCH):
        r.mm(B.pM[0:1, :], B.onesD[:], B.t1[:, c, :], start=(c == 0), stop=(c == NCH - 1),
             inc=("pe", 1) if c == NCH - 1 else None)
    r.wait("vector", "pe")
    r.cp(B.rows[:, 0, :], B.pQ[0:1, :])
    r.cp(B.rows[:, 1, :], B.pM[0:1, :])
    r.tt(B.rows[:, 2, :], B.rows[:, 0, :], B.rows[:, 0, :], op=ALU.mult)
    r.tt(B.rows[:, 2, :], B.rows[:, 1, :], B.rows[:, 2, :], op=ALU.subtract, inc=("dve", 1))
    r.wait("scalar", "dve")
    r.act(B.rows[:, 2, :], B.rows[:, 2, :], AF.Ln, bias=B.e11[:], scale=1.0)
    r.act(B.rows[:, 2, :], B.rows[:, 2, :], AF.Exp, bias=B.z11[:], scale=-0.5, inc=("act", 1))
    r.wait("vector", "act")
    r.tt(B.rows[:, 1, :], B.rows[:, 0, :], B.rows[:, 2, :], op=ALU.mult, inc=("dve", 1))
    r.wait("tensor", "act")
    r.wait("tensor", "dve")
    r.mm(B.pSC[:, 0:512], B.onesf[:], B.rows[:, 2, :], inc=("pe", 1))
    r.mm(B.pSC[:, 512:1024], B.negf[:], B.rows[:, 1, :], inc=("pe", 1))
    r.wait("vector", "pe")
    r.cp(B.rbs[:], B.pSC[:, 0:512])
    r.cp(B.abs_[:], B.pSC[:, 512:1024], inc=("dve", 1))
    rb = B.rbs[:].rearrange("p (o t) -> p o t", o=1)
    ab = B.abs_[:].rearrange("p (o t) -> p o t", o=1)
    r.ttb(B.t1[:], src3, rb, op=ALU.mult)
    if resid is None:
        r.ttb(dst3, B.t1[:], ab, op=ALU.add, inc=("dve", 1))
    else:
        r.ttb(B.t2[:], B.t1[:], ab, op=ALU.add)
        r.wait("vector", "dx")
        r.tt(B.xres[:], B.xres[:], B.t2[:], op=ALU.add, inc=("dve", 1))
        if store is not None:
            r.wait("sync", "dve")
            r.dma("sync", out=store, in_=B.xres[:], inc=("dx", 16))


def dram_grp(dram, seq, g, w=GRP):
    c0 = seq * S + g * GRP
    xr = dram.rearrange("(c p) t -> p c t", p=128)
    return xr[:, :, c0:c0 + w]


def wslice_ap(wdram, m, kw=NCH):
    wr = wdram.rearrange("(c p) o -> p c o", p=128)
    return wr[:, :, m * 128:(m + 1) * 128]


def emit_kv_phase(ctx, B, seq, L):
    wq = "Wq0" if L == 0 else "wq1t"
    wk = "Wk0" if L == 0 else "wk1t"
    wv = "Wv0" if L == 0 else "wv1t"
    bkc = ccol(B, "bk0" if L == 0 else "bk1", 8)
    nkk = 2 if L == 0 else NCH

    def kv_flow(r):
        if L == 0:
            for m2 in range(2):
                slot = m2 % 4
                r.dma("gpsimd", out=B.wsl[:, slot, :, :], in_=wslice_ap(ctx.dram["Wl0"], m2),
                      inc=("dw", 16))
                r.wait("tensor", "dw")
                for g in range(NG):
                    ps = [B.pSC, B.pSC2][g % 2][:, 0:512]
                    r.wait("tensor", "act", ctx.cnt["act"])
                    for k in range(NCH):
                        r.mm(ps, B.wsl[:, slot, k, :], B.xn[:, k, 1 + g * GRP:1 + (g + 1) * GRP],
                             start=(k == 0), stop=(k == NCH - 1),
                             inc=("pe", 1) if k == NCH - 1 else None)
                    r.wait("scalar", "pe")
                    r.act(B.latT[:, m2, g * GRP:(g + 1) * GRP], ps, AF.Identity,
                          bias=ccol(B, "bl0", 2)[:, m2:m2 + 1], scale=1.0, inc=("act", 1))
        for m in range(NPAIR):
            slot = m % 4
            r.wait("gpsimd", "act")
            r.dma("gpsimd", out=B.wsl[:, slot, :, 0:128] if nkk == NCH else B.wsl[:, slot, 0:nkk, :],
                  in_=wslice_ap(ctx.dram[wk], m, nkk), inc=("dw", 16))
            r.wait("tensor", "dw")
            for g in range(NG):
                ps = [B.pSC, B.pSC2][g % 2][:, 0:512]
                r.wait("tensor", "act", ctx.cnt["act"])
                for k in range(nkk):
                    rhs = (B.latT[:, k, g * GRP:(g + 1) * GRP] if L == 0 else
                           B.xn[:, k, 1 + g * GRP:1 + (g + 1) * GRP])
                    r.mm(ps, B.wsl[:, slot, k, :], rhs, start=(k == 0), stop=(k == nkk - 1),
                         inc=("pe", 1) if k == nkk - 1 else None)
                r.wait("scalar", "pe")
                r.act(B.kT[:, m, g * GRP:(g + 1) * GRP], ps, AF.Identity,
                      bias=bkc[:, m:m + 1], scale=1.0, inc=("act", 1))
            if L == 1:
                row = B.kT[:, m, :]
                r.wait("tensor", "act")
                for gg in range(4):
                    ps = [B.pSC, B.pSC2][gg // 2][:, (gg % 2) * 512:(gg % 2) * 512 + 512]
                    r.mm(ps, B.Pswp[:], row[:, gg * 512:(gg + 1) * 512], inc=("pe", 1))
                r.wait("vector", "act")
                r.tt(B.rtmp[:], row, B.cosT[:], op=ALU.mult)
                r.wait("vector", "pe")
                for gg in range(4):
                    ps = [B.pSC, B.pSC2][gg // 2][:, (gg % 2) * 512:(gg % 2) * 512 + 512]
                    r.tt(row[:, gg * 512:(gg + 1) * 512], ps,
                         B.sinT[:, gg * 512:(gg + 1) * 512], op=ALU.mult)
                r.tt(row, row, B.rtmp[:], op=ALU.add, inc=("dve", 1))
                r.wait("tensor", "dve")
        for half in range(2):
            r.wait("gpsimd", "dve")
            r.wait("gpsimd", "act")
            r.wait("gpsimd", "pe")
            r.dma("gpsimd", out=B.wrhs[:, 0:nkk, :],
                  in_=ctx.dram[wv].rearrange("(c p) o -> p c o", p=128)[:, :, half * 512:(half + 1) * 512],
                  inc=("dw", 16))
            r.wait("tensor", "dw")
            for jc in range(NJC):
                ps = [B.pSC, B.pSC2][jc % 2][:, 0:512]
                r.wait("tensor", "dve", ctx.cnt["dve"])
                for k in range(nkk):
                    lhsT = (B.latT[:, k, jc * 128:(jc + 1) * 128] if L == 0 else
                            B.xn[:, k, 1 + jc * 128:1 + (jc + 1) * 128])
                    r.mm(ps, lhsT, B.wrhs[:, k, :], start=(k == 0), stop=(k == nkk - 1),
                         inc=("pe", 1) if k == nkk - 1 else None)
                r.wait("vector", "pe")
                vv = B.v_s[:, jc, :].rearrange("p (h c) -> p h c", c=65)
                bv = B.Bv[:, half * 512:(half + 1) * 512].rearrange("p (h c) -> p h c", c=64)
                r.tt(vv[:, half * 8:(half + 1) * 8, 0:64],
                     ps.rearrange("p (h c) -> p h c", c=64), bv, op=ALU.add, inc=("dve", 1))
    ctx.phase(f"kv_a{L}_s{seq}", kv_flow)


def emit_attn(ctx, B, seq, L, src, dst):
    wq = "Wq0" if L == 0 else "wq1t"
    wk = "Wk0" if L == 0 else "wk1t"
    wv = "Wv0" if L == 0 else "wv1t"
    wo = "Wo0" if L == 0 else "wo1t"
    bqc = ccol(B, "bq0" if L == 0 else "bq1", 8)
    bkc = ccol(B, "bk0" if L == 0 else "bk1", 8)
    boc = ccol(B, "bo0" if L == 0 else "bo1", 8)
    nkk = 2 if L == 0 else NCH
    bvrow = "bv0row" if L == 0 else "bv1row"

    if L == 1:
        emit_prep_tables(ctx, B, tag=f"_s{seq}")

    def bv_flow(r):
        r.wait("sync", "dx")
        r.wait("sync", "pe")
        r.wait("sync", "dve")
        r.dma("sync", out=B.brow, in_=ctx.dram[bvrow], inc=("dx", 16))
        r.wait("tensor", "dx")
        r.mm(B.pSC[:, 0:512], B.onesf[:], B.rows[0:1, 0, :], inc=("pe", 1))
        r.mm(B.pSC[:, 512:1024], B.onesf[:], B.rows[0:1, 1, :], inc=("pe", 1))
        r.wait("vector", "pe")
        r.cp(B.Bv[:, 0:512], B.pSC[:, 0:512])
        r.cp(B.Bv[:, 512:1024], B.pSC[:, 512:1024], inc=("dve", 1))
    ctx.phase(f"bv_a{L}_s{seq}", bv_flow)

    def ln1_flow(r):
        for g in range(NG):
            r.wait("gpsimd", "dx")
            r.wait("gpsimd", "pe")
            r.wait("gpsimd", "dve")
            r.dma("gpsimd", out=B.xb[:], in_=dram_grp(ctx.dram[src], seq, g), inc=("dw", 16))
            emit_ln(ctx, r, B, B.xb[:], B.xn[:, :, 1 + g * GRP: 1 + (g + 1) * GRP])
    ctx.phase(f"ln1_a{L}_s{seq}", ln1_flow)

    emit_kv_phase(ctx, B, seq, L)

    def attn_flow(r):
        for g in range(NG):
            gc = slice(1 + g * GRP, 1 + (g + 1) * GRP)
            tc = slice(g * GRP, (g + 1) * GRP)
            r.wait("sync", "dve")
            r.wait("sync", "dx")
            r.dma("sync", out=B.xres[:], in_=dram_grp(ctx.dram[src], seq, g), inc=("dx", 16))
            for m in range(NPAIR):
                slot = (g * NPAIR + m) % 4
                r.wait("gpsimd", "pe")
                r.dma("gpsimd", out=B.wsl[:, slot, :, :], in_=wslice_ap(ctx.dram[wq], m),
                      inc=("dw", 16))
                r.wait("tensor", "dw")
                r.wait("tensor", "dve", ctx.cnt["dve"])
                for k in range(NCH):
                    r.mm(B.pQ[:], B.wsl[:, slot, k, :], B.xn[:, k, gc],
                         start=(k == 0), stop=(k == NCH - 1),
                         inc=("pe", 1) if k == NCH - 1 else None)
                r.wait("scalar", "pe")
                r.act(B.qp[:], B.pQ[:], AF.Identity, bias=bqc[:, m:m + 1], scale=1.0,
                      inc=("act", 1))
                if L == 1:
                    r.wait("tensor", "act")
                    r.mm(B.pM[:], B.Pswp[:], B.qp[:], inc=("pe", 1))
                    r.wait("vector", "act")
                    r.tt(B.qtmp[:], B.qp[:], B.cosT[:, tc], op=ALU.mult)
                    r.wait("vector", "pe")
                    r.tt(B.qp[:], B.pM[:], B.sinT[:, tc], op=ALU.mult)
                    r.tt(B.qp[:], B.qp[:], B.qtmp[:], op=ALU.add, inc=("dve", 1))
                    r.wait("tensor", "dve")
                else:
                    r.wait("tensor", "act")
                pe0 = ctx.cnt["pe"]
                act0 = ctx.cnt["act"]
                for j in range(NJC):
                    psc = [B.pSC, B.pSC2][j % 2]
                    r.wait("tensor", "act", act0 + (j - 1))
                    r.mm(psc[:, 0:512], B.kT[0:64, m, j * 128:(j + 1) * 128], B.qp[0:64, :],
                         tile_position=(0, 0))
                    r.mm(psc[:, 512:1024], B.kT[64:128, m, j * 128:(j + 1) * 128],
                         B.qp[64:128, :], tile_position=(64, 0), inc=("pe", 1))
                    r.wait("scalar", "pe", pe0 + 2 * j + 1)
                    if j >= 4:
                        r.wait("scalar", "pe", pe0 + 2 * (j - 4) + 2)
                    r.act(B.expb[:, j % 4, :], psc[:], AF.Exp, bias=ccol(B, "eshift"),
                          scale=SCALE, inc=("act", 1))
                    r.wait("tensor", "act", act0 + j + 1)
                    r.mm(B.pavA[0:65, :], B.v_s[:, j, (2 * m) * 65:(2 * m) * 65 + 65],
                         B.expb[:, j % 4, 0:512], start=(j == 0), stop=(j == NJC - 1))
                    r.mm(B.pavB[0:65, :], B.v_s[:, j, (2 * m + 1) * 65:(2 * m + 1) * 65 + 65],
                         B.expb[:, j % 4, 512:1024], start=(j == 0), stop=(j == NJC - 1),
                         inc=("pe", 1))
                r.wait("vector", "pe")
                r.cp(B.avu[0:64, 0, :], B.pavA[0:64, :])
                r.cp(B.dsb[64:65, 0, :], B.pavA[64:65, :])
                r.cp(B.avu[0:64, 1, :], B.pavB[0:64, :])
                r.cp(B.dsb[64:65, 1, :], B.pavB[64:65, :])
                r.recip(B.dsb[64:65, 0, :], B.dsb[64:65, 0, :])
                r.recip(B.dsb[64:65, 1, :], B.dsb[64:65, 1, :], inc=("dve", 1))
                r.wait("tensor", "dve")
                r.mm(B.pQ[0:64, :], B.selb[:], B.dsb[:, 0, :], inc=("pe", 1))
                r.mm(B.pM[0:64, :], B.selb[:], B.dsb[:, 1, :], inc=("pe", 1))
                r.wait("vector", "pe")
                r.tt(B.ao[0:64, m, :], B.avu[0:64, 0, :], B.pQ[0:64, :], op=ALU.mult)
                r.tt(B.qtmp[0:64, :], B.avu[0:64, 1, :], B.pM[0:64, :],
                     op=ALU.mult, inc=("dve", 1))
                r.wait("tensor", "dve")
                r.mm(B.pavA[64:128, :], B.I64[:], B.qtmp[0:64, :], tile_position=(0, 64),
                     inc=("pe", 1))
                r.wait("vector", "pe")
                r.cp(B.ao[64:128, m, :], B.pavA[64:128, :], inc=("dve", 1))
            for m2 in range(NCH):
                slot = m2 % 4
                r.wait("gpsimd", "pe")
                r.dma("gpsimd", out=B.wsl[:, slot, :, :], in_=wslice_ap(ctx.dram[wo], m2),
                      inc=("dw", 16))
                ps = [B.pSC, B.pSC2][m2 % 2][:, 0:512]
                r.wait("tensor", "dw")
                r.wait("tensor", "act", ctx.cnt["act"])
                r.wait("tensor", "dve", ctx.cnt["dve"])
                for k in range(NCH):
                    r.mm(ps, B.wsl[:, slot, k, :], B.ao[:, k, :], start=(k == 0),
                         stop=(k == NCH - 1), inc=("pe", 1) if k == NCH - 1 else None)
                r.wait("scalar", "pe")
                r.act(B.yb[:, m2, :], ps, AF.Identity, bias=boc[:, m2:m2 + 1], scale=1.0,
                      inc=("act", 1))
            r.wait("vector", "act")
            emit_ln(ctx, r, B, B.yb[:], None, resid=True,
                    store=dram_grp(ctx.dram[dst], seq, g))
    ctx.phase(f"attn_a{L}_s{seq}", attn_flow)


def emit_ffn0(ctx, B, seq, src, dst):
    def flow(r):
        for g in range(NG):
            r.wait("gpsimd", "dx")
            r.wait("gpsimd", "pe")
            r.wait("gpsimd", "dve")
            r.dma("gpsimd", out=B.xb[:], in_=dram_grp(ctx.dram[src], seq, g), inc=("dw", 16))
            emit_ln(ctx, r, B, B.xb[:], B.xn[:, :, 1 + g * GRP:1 + (g + 1) * GRP])
            r.wait("sync", "dve")
            r.wait("sync", "dx")
            r.dma("sync", out=B.xres[:], in_=dram_grp(ctx.dram[src], seq, g), inc=("dx", 16))
            for m2 in range(NCH):
                slot = m2 % 4
                r.wait("sync", "act")
                r.dma("sync", out=B.wsl[:, slot, :, :], in_=wslice_ap(ctx.dram["tdram"], m2),
                      inc=("dx", 16))
                ps = [B.pSC, B.pSC2][m2 % 2][:, 0:512]
                r.wait("tensor", "dx")
                r.wait("tensor", "act", ctx.cnt["act"])
                r.wait("tensor", "dve", ctx.cnt["dve"])
                for k in range(NCH):
                    r.mm(ps, B.wsl[:, slot, k, :], B.xn[:, k, 1 + g * GRP:1 + (g + 1) * GRP],
                         start=(k == 0), stop=(k == NCH - 1),
                         inc=("pe", 1) if k == NCH - 1 else None)
                r.wait("scalar", "pe")
                r.act(B.yb[:, m2, :], ps, AF.Identity, bias=ccol(B, "fb", 8)[:, m2:m2 + 1],
                      scale=ccol(B, "alpha"), inc=("act", 1))
            r.wait("vector", "act")
            emit_ln(ctx, r, B, B.yb[:], None, resid=True,
                    store=dram_grp(ctx.dram[dst], seq, g))
    ctx.phase(f"ffn0_s{seq}", flow)


def emit_ffn1(ctx, B, seq, src, dst):
    def flow(r):
        for g in range(NG):
            gc = slice(1 + g * GRP, 1 + (g + 1) * GRP)
            pc = slice(g * GRP, g * GRP + GRP)
            r.wait("gpsimd", "dx")
            r.wait("gpsimd", "pe")
            r.wait("gpsimd", "dve")
            r.dma("gpsimd", out=B.xb[:], in_=dram_grp(ctx.dram[src], seq, g), inc=("dw", 16))
            emit_ln(ctx, r, B, B.xb[:], B.xn[:, :, gc])
            r.wait("sync", "dve")
            r.wait("sync", "dx")
            r.dma("sync", out=B.xres[:], in_=dram_grp(ctx.dram[src], seq, g), inc=("dx", 16))
            r.wait("vector", "dve", ctx.cnt["dve"])
            r.wait("vector", "pe", ctx.cnt["pe"])
            for c in range(NCH):
                r.ts(B.t1[:, c, :], B.xn[:, c, gc], ccol(B, "mur", 8)[:, c:c + 1], None,
                     op0=ALU.mult)
                r.ts(B.qtmp[:], B.xn[:, c, pc], ccol(B, "murc", 8)[:, c:c + 1], None,
                     op0=ALU.mult)
                r.tt(B.t1[:, c, :], B.t1[:, c, :], B.qtmp[:], op=ALU.add)
                r.ts(B.t2[:, c, :], B.xn[:, c, gc], ccol(B, "muk", 8)[:, c:c + 1], None,
                     op0=ALU.mult)
                r.ts(B.qtmp[:], B.xn[:, c, pc], ccol(B, "mukc", 8)[:, c:c + 1], None,
                     op0=ALU.mult)
                r.tt(B.t2[:, c, :], B.t2[:, c, :], B.qtmp[:], op=ALU.add,
                     inc=("dve", 1) if c == NCH - 1 else None)
            for m2 in range(NCH):
                slot = m2 % 4
                r.wait("gpsimd", "act")
                r.dma("gpsimd", out=B.wsl[:, slot, :, :], in_=wslice_ap(ctx.dram["wr1t"], m2),
                      inc=("dw", 16))
                ps = [B.pSC, B.pSC2][m2 % 2][:, 0:512]
                r.wait("tensor", "dw")
                r.wait("tensor", "dve", ctx.cnt["dve"])
                r.wait("tensor", "act", ctx.cnt["act"])
                for k in range(NCH):
                    r.mm(ps, B.wsl[:, slot, k, :], B.t1[:, k, :], start=(k == 0),
                         stop=(k == NCH - 1), inc=("pe", 1) if k == NCH - 1 else None)
                r.wait("scalar", "pe")
                r.act(B.fr[:, m2, :], ps, AF.Sigmoid, bias=ccol(B, "zero"), scale=1.0,
                      inc=("act", 1))
            for m2 in range(NCH):
                slot = m2 % 4
                r.wait("gpsimd", "act")
                r.wait("gpsimd", "dve")
                r.dma("gpsimd", out=B.wsl[:, slot, :, :], in_=wslice_ap(ctx.dram["wk2t"], m2),
                      inc=("dw", 16))
                ps = [B.pSC, B.pSC2][m2 % 2][:, 0:512]
                r.wait("tensor", "dw")
                r.wait("tensor", "dve", ctx.cnt["dve"])
                r.wait("tensor", "act", ctx.cnt["act"])
                for k in range(NCH):
                    r.mm(ps, B.wsl[:, slot, k, :], B.t2[:, k, :], start=(k == 0),
                         stop=(k == NCH - 1), inc=("pe", 1) if k == NCH - 1 else None)
                r.wait("vector", "pe")
                r.ts(B.qtmp[:], ps, 0.0, None, op0=ALU.max)
                r.tt(B.fk[:, m2, :], B.qtmp[:], B.qtmp[:], op=ALU.mult, inc=("dve", 1))
            for m2 in range(NCH):
                slot = m2 % 4
                r.wait("gpsimd", "dve")
                r.dma("gpsimd", out=B.wsl[:, slot, :, :], in_=wslice_ap(ctx.dram["wv2t"], m2),
                      inc=("dw", 16))
                ps = [B.pSC, B.pSC2][m2 % 2][:, 0:512]
                r.wait("tensor", "dw")
                r.wait("tensor", "dve", ctx.cnt["dve"])
                for k in range(NCH):
                    r.mm(ps, B.wsl[:, slot, k, :], B.fk[:, k, :], start=(k == 0),
                         stop=(k == NCH - 1), inc=("pe", 1) if k == NCH - 1 else None)
                r.wait("vector", "pe")
                r.tt(B.yb[:, m2, :], B.fr[:, m2, :], ps, op=ALU.mult, inc=("dve", 1))
            r.wait("vector", "dve", ctx.cnt["dve"])
            emit_ln(ctx, r, B, B.yb[:], None, resid=True,
                    store=dram_grp(ctx.dram[dst], seq, g))
    ctx.phase(f"ffn1_s{seq}", flow)


def emit_prep_consts(ctx, B, need):
    d = ctx.dram

    def flow(r):
        r.ms(B.onesb[:], 1.0)
        r.ms(B.onesf[:], 1.0)
        r.ms(B.negf[:], -1.0)
        r.ms(B.onesD[:], 1.0 / 1024.0)
        r.ms(B.onesP[:], 1.0)
        r.ms(B.e11[:], EPS)
        r.ms(B.z11[:], 0.0)
        r.ms(B.cols[:], 0.0)
        r.ms(ccol(B, "eshift"), -ESHIFT)
        r.ms(B.xn[:, :, 0:1], 0.0)
        r.ms(B.selb[:], 0.0)
        r.ms(B.selb[64:65, :], 1.0)
        vv = B.v_s[:].rearrange("p j (h c) -> p j h c", c=65)
        r.ms(vv[:, :, :, 64:65], 1.0, inc=("pool", 1))
        r.wait("sync", "pool")
        loads = []
        if SB_A0 in need:
            loads += [("bq0pc", "bq0", 8), ("bk0pc", "bk0", 8), ("bo0pc", "bo0", 8),
                      ("bl0pc", "bl0", 2)]
        if SB_F0 in need:
            loads += [("fb0pc", "fb", 8)]
        if SB_A1 in need:
            loads += [("bq1pc", "bq1", 8), ("bk1pc", "bk1", 8), ("bo1pc", "bo1", 8)]
        if SB_F1 in need:
            loads += [("murpc", "mur", 8), ("mukpc", "muk", 8)]
        for nm, cn, w in loads:
            r.dma("sync", out=ccol(B, cn, w), in_=d[nm], inc=("dx", 16))
        if SB_F0 in need:
            r.dma("sync", out=B.rows[0:1, 0, 0:1], in_=d["alpha"], inc=("dx", 16))
            r.wait("tensor", "dx")
            r.mm(B.pQ[:, 0:1], B.onesf[:], B.rows[0:1, 0, 0:1], inc=("pe", 1))
            r.wait("vector", "pe")
            r.cp(ccol(B, "alpha"), B.pQ[:, 0:1], inc=("dve", 1))
        if SB_F1 in need:
            r.wait("vector", "dx")
            r.ts(ccol(B, "murc", 8), ccol(B, "mur", 8), -1.0, 1.0, op0=ALU.mult, op1=ALU.add)
            r.ts(ccol(B, "mukc", 8), ccol(B, "muk", 8), -1.0, 1.0, op0=ALU.mult, op1=ALU.add,
                 inc=("dve", 1))
        if SB_A1 in need:
            r.iota(B.scrI[:, 0:1], [[1, 1]], cm=1, inc=("pool", 1))
            r.wait("vector", "pool")
            r.ts(B.scrI[:, 1:2], B.scrI[:, 0:1], 32, None, op0=ALU.bitwise_xor)
            r.cp(ccol(B, "pf"), B.scrI[:, 0:1])
            r.cp(ccol(B, "pxorf"), B.scrI[:, 1:2], inc=("dve", 1))
            r.iota(B.rows[0:1, 1, 0:128], [[1, 128]], cm=0, inc=("pool", 1))
            r.wait("tensor", "pool")
            r.wait("tensor", "dve", ctx.cnt["dve"])
            r.mm(B.pM[:, 0:128], B.onesf[:], B.rows[0:1, 1, 0:128], inc=("pe", 1))
            r.wait("vector", "pe")
            r.ts(B.scrA[:, 0:128], B.pM[:, 0:128], ccol(B, "pxorf"), None, op0=ALU.subtract)
            r.tt(B.scrA[:, 0:128], B.scrA[:, 0:128], B.scrA[:, 0:128], op=ALU.mult)
            r.ts(B.scrA[:, 0:128], B.scrA[:, 0:128], 1.0, None, op0=ALU.min)
            r.ts(B.Pswp[:], B.scrA[:, 0:128], -1.0, 1.0, op0=ALU.mult, op1=ALU.add)
            r.ts(B.scrA[:, 0:128], B.pM[:, 0:128], ccol(B, "pf"), None, op0=ALU.subtract)
            r.tt(B.scrA[:, 0:128], B.scrA[:, 0:128], B.scrA[:, 0:128], op=ALU.mult)
            r.ts(B.scrA[:, 0:128], B.scrA[:, 0:128], 1.0, None, op0=ALU.min)
            r.ts(B.scrB[:, 0:128], B.scrA[:, 0:128], -1.0, 1.0, op0=ALU.mult, op1=ALU.add)
            r.cp(B.I64[:], B.scrB[0:64, 0:64], inc=("dve", 1))
        if SB_A0 in need and SB_A1 not in need:
            r.iota(B.scrI[:, 0:1], [[1, 1]], cm=1, inc=("pool", 1))
            r.wait("vector", "pool")
            r.cp(ccol(B, "pf"), B.scrI[:, 0:1], inc=("dve", 1))
            r.iota(B.rows[0:1, 1, 0:128], [[1, 128]], cm=0, inc=("pool", 1))
            r.wait("tensor", "pool")
            r.wait("tensor", "dve", ctx.cnt["dve"])
            r.mm(B.pM[:, 0:128], B.onesf[:], B.rows[0:1, 1, 0:128], inc=("pe", 1))
            r.wait("vector", "pe")
            r.ts(B.scrA[:, 0:128], B.pM[:, 0:128], ccol(B, "pf"), None, op0=ALU.subtract)
            r.tt(B.scrA[:, 0:128], B.scrA[:, 0:128], B.scrA[:, 0:128], op=ALU.mult)
            r.ts(B.scrA[:, 0:128], B.scrA[:, 0:128], 1.0, None, op0=ALU.min)
            r.ts(B.scrB[:, 0:128], B.scrA[:, 0:128], -1.0, 1.0, op0=ALU.mult, op1=ALU.add)
            r.cp(B.I64[:], B.scrB[0:64, 0:64], inc=("dve", 1))
    ctx.phase("prep_consts", flow)


def emit_prep_tables(ctx, B, tag=""):
    LN1E4_32 = float(np.log(10000.0) / 32.0)
    INV2PI = float(1.0 / (2 * np.pi))
    TWOPI = float(2 * np.pi)

    def flow1(r):
        r.wait("vector", "dx")
        r.wait("vector", "dw")
        r.wait("gpsimd", "dx")
        r.iota(B.scrI[:, 0:1], [[1, 1]], cm=1, inc=("pool", 1))
        r.wait("vector", "pool")
        r.ts(B.scrI[:, 2:3], B.scrI[:, 0:1], 32, None, op0=ALU.bitwise_and)
        r.ts(B.scrI[:, 1:2], B.scrI[:, 0:1], 31, None, op0=ALU.bitwise_and)
        r.cp(B.scrB[:, 0:1], B.scrI[:, 2:3])
        r.cp(B.scrA[:, 0:1], B.scrI[:, 1:2], inc=("dve", 1))
        r.wait("scalar", "dve")
        r.act(ccol(B, "absacc"), B.scrA[:, 0:1], AF.Exp, bias=ccol(B, "zero"), scale=-LN1E4_32,
              inc=("act", 1))
        r.wait("vector", "act")
        r.ts(ccol(B, "pxorf"), B.scrB[:, 0:1], 1.0 / 16.0, -1.0, op0=ALU.mult, op1=ALU.add,
             inc=("dve", 1))
    ctx.phase(f"prep_tab1{tag}", flow1)

    def flow2b(r):
        r.iota(B.rows[0:1, 1, :], [[1, GRP]], cm=0, inc=("pool", 1))
        for cc in range(4):
            base = float(cc * GRP)
            sl = slice(cc * GRP, (cc + 1) * GRP)
            r.wait("tensor", "pool")
            r.wait("tensor", "dve", ctx.cnt["dve"])
            r.mm(B.pQ[:], B.onesf[:], B.rows[0:1, 1, :], inc=("pe", 1))
            r.wait("vector", "pe")
            r.ts(B.scrC[:], B.pQ[:], base, None, op0=ALU.add)
            r.ts(B.scrC[:], B.scrC[:], ccol(B, "absacc"), None, op0=ALU.mult)
            r.ts(B.scrB[:], B.scrC[:], INV2PI, None, op0=ALU.mult)
            r.cp(B.scrI[:], B.scrB[:])
            r.cp(B.scrA[:], B.scrI[:])
            r.tt(B.scrB[:], B.scrB[:], B.scrA[:], op=ALU.subtract, inc=("dve", 1))
            r.wait("scalar", "dve")
            r.act(B.scrB[:], B.scrB[:], AF.Sin, bias=ccol(B, "zero"), scale=TWOPI, inc=("act", 1))
            r.wait("vector", "act")
            r.ts(B.sinT[:, sl], B.scrB[:], ccol(B, "pxorf"), None, op0=ALU.mult)
            r.ts(B.scrB[:], B.scrC[:], INV2PI, 0.25, op0=ALU.mult, op1=ALU.add)
            r.cp(B.scrI[:], B.scrB[:])
            r.cp(B.scrA[:], B.scrI[:])
            r.tt(B.scrB[:], B.scrB[:], B.scrA[:], op=ALU.subtract, inc=("dve", 1))
            r.wait("scalar", "dve")
            r.act(B.scrB[:], B.scrB[:], AF.Sin, bias=ccol(B, "zero"), scale=TWOPI, inc=("act", 1))
            r.wait("vector", "act")
            r.cp(B.cosT[:, sl], B.scrB[:], inc=("dve", 1))
    ctx.phase(f"prep_tab2{tag}", flow2b)


def emit_prep_ternary(ctx, B):
    d = ctx.dram
    wr = d["fw0t"].rearrange("(c p) o -> p c o", p=128)
    tr = d["tdram"].rearrange("(c p) o -> p c o", p=128)

    def flow(r):
        for c in range(NCH):
            r.wait("sync", "dve", ctx.cnt["dve"])
            r.dma("sync", out=B.scrW[:], in_=wr[:, c, :], inc=("dx", 16))
            r.wait("vector", "dx")
            r.red(ccol(B, "absacc", 8)[:, c:c + 1], B.scrW[:], ALU.add, absval=True,
                  inc=("dve", 1))
        r.cp(B.scrB[:], B.scrW[:, 0:512])
        r.red(B.dcol[:], ccol(B, "absacc", 8), ALU.add)
        r.cp(B.scrB[:], B.scrW[:, 0:512])
        r.tt(B.scrB[:], B.scrB[:], B.scrB[:], op=ALU.mult, inc=("dve", 1))
        r.wait("tensor", "dve")
        r.mm(B.pQ[0:1, 0:1], B.dcol[:], B.onesP[:], inc=("pe", 1))
        r.wait("vector", "pe")
        r.ts(B.rows[0:1, 0, 0:1], B.pQ[0:1, 0:1], 0.7 / (1024.0 * 1024.0), None,
             op0=ALU.mult, inc=("dve", 1))
        r.wait("tensor", "dve")
        r.mm(B.pQ[:, 0:1], B.onesf[:], B.rows[0:1, 0, 0:1], inc=("pe", 1))
        r.wait("vector", "pe")
        r.cp(ccol(B, "delta"), B.pQ[:, 0:1], inc=("dve", 1))
        for c in range(NCH):
            r.wait("sync", "dx", ctx.cnt["dx"])
            r.wait("sync", "dve", ctx.cnt["dve"])
            r.wait("sync", "dw", ctx.cnt["dw"])
            r.dma("sync", out=B.scrW[:], in_=wr[:, c, :], inc=("dx", 16))
            r.wait("scalar", "dx")
            r.act(B.rtmp[:, 0:1024], B.scrW[:], AF.Sign, bias=ccol(B, "zero"), scale=1.0)
            r.act(B.scrW2[:], B.scrW[:], AF.Abs, bias=ccol(B, "zero"), scale=1.0,
                  inc=("act", 1))
            r.wait("vector", "act")
            r.ts(B.scrW[:], B.scrW2[:], ccol(B, "delta"), 0.0, op0=ALU.subtract,
                 op1=ALU.max, inc=("dve", 1))
            r.wait("scalar", "dve")
            r.act(B.scrW2[:], B.scrW[:], AF.Sign, bias=ccol(B, "zero"), scale=1.0,
                  inc=("act", 1))
            r.wait("vector", "act")
            r.tt(B.scrW[:], B.rtmp[:, 0:1024], B.scrW2[:], op=ALU.mult, inc=("dve", 1))
            r.wait("gpsimd", "dve")
            r.dma("gpsimd", out=tr[:, c, :], in_=B.scrW[:], inc=("dw", 16))
        r.wait("sync", "dw")
    ctx.phase("prep_tern", flow)


WEIGHT_SPECS = {
    "Wq0": [D, D], "Wl0": [D, LAT], "Wk0": [LAT, D], "Wv0": [LAT, D], "Wo0": [D, D],
    "fw0t": [D, D],
    "wq1t": [D, D], "wk1t": [D, D], "wv1t": [D, D], "wo1t": [D, D],
    "wr1t": [D, D], "wk2t": [D, D], "wv2t": [D, D],
}
PCOL_SPECS = {
    "bq0pc": [128, 8], "bk0pc": [128, 8], "bo0pc": [128, 8], "bl0pc": [128, 2],
    "fb0pc": [128, 8], "bq1pc": [128, 8], "bk1pc": [128, 8], "bo1pc": [128, 8],
    "murpc": [128, 8], "mukpc": [128, 8],
}
ROW_SPECS = {"bv0row": [1, D], "bv1row": [1, D], "alpha": [1, 1]}

SB_WEIGHTS = {
    SB_A0: ["Wq0", "Wl0", "Wk0", "Wv0", "Wo0", "bq0pc", "bk0pc", "bo0pc", "bl0pc", "bv0row"],
    SB_F0: ["fw0t", "fb0pc", "alpha"],
    SB_A1: ["wq1t", "wk1t", "wv1t", "wo1t", "bq1pc", "bk1pc", "bo1pc", "bv1row"],
    SB_F1: ["wr1t", "wk2t", "wv2t", "murpc", "mukpc"],
}


def build_program(parts):
    nc = bass.Bass("TRN2", target_bir_lowering=False)
    dram = {}
    dram["x"] = nc.dram_tensor("x", [D, T], F32, kind="ExternalInput").ap()
    dram["y"] = nc.dram_tensor("y", [D, T], F32, kind="ExternalOutput").ap()
    if len(parts) > 1:
        dram["xbuf"] = nc.dram_tensor("xbuf", [D, T], F32, kind="Internal").ap()
    if SB_F0 in parts:
        dram["tdram"] = nc.dram_tensor("tdram", [D, D], BF16, kind="Internal").ap()
    names = set()
    for p in parts:
        names.update(SB_WEIGHTS[p])
    for n in sorted(names):
        if n in WEIGHT_SPECS:
            dram[n] = nc.dram_tensor(n, WEIGHT_SPECS[n], F32, kind="ExternalInput").ap()
        elif n in PCOL_SPECS:
            dram[n] = nc.dram_tensor(n, PCOL_SPECS[n], F32, kind="ExternalInput").ap()
        else:
            dram[n] = nc.dram_tensor(n, ROW_SPECS[n], F32, kind="ExternalInput").ap()

    with ExitStack() as stack:
        E = stack.enter_context
        B = alloc_buffers(nc, stack)
        sems = {}
        for s in ["dx", "dw", "pe", "act", "dve", "pool"]:
            sems[s] = E(nc.semaphore(s))
        ctx = Ctx(nc, sems)
        ctx.dram = dram

        emit_prep_consts(ctx, B, set(parts))
        if SB_F0 in parts:
            emit_prep_ternary(ctx, B)

        for i, p in enumerate(parts):
            src = "x" if i == 0 else "xbuf"
            dst = "y" if i == len(parts) - 1 else "xbuf"
            for seq in range(NSEQ):
                if p == SB_A0:
                    emit_attn(ctx, B, seq, 0, src, dst)
                elif p == SB_F0:
                    emit_ffn0(ctx, B, seq, src, dst)
                elif p == SB_A1:
                    emit_attn(ctx, B, seq, 1, src, dst)
                elif p == SB_F1:
                    emit_ffn1(ctx, B, seq, src, dst)

        def fin(r):
            r.wait("sync", "dx")
            r.wait("sync", "dw")
        ctx.phase("fin", fin)
    return nc


def _pcol(b):
    return np.ascontiguousarray(np.asarray(b, np.float32).reshape(-1, 128).T)


def _prep_shared(inputs):
    f = lambda a: np.ascontiguousarray(np.asarray(a, np.float32))
    tr = lambda a: np.ascontiguousarray(np.asarray(a, np.float32).T)
    sh = {
        "Wq0": f(inputs["L0_attn_Wq"]), "Wl0": f(inputs["L0_attn_Wl"]),
        "Wk0": f(inputs["L0_attn_Wk"]), "Wv0": f(inputs["L0_attn_Wv"]),
        "Wo0": f(inputs["L0_attn_Wo"]),
        "fw0t": tr(inputs["L0_ffn_weight"]),
        "wq1t": tr(inputs["L1_attn_wq"]), "wk1t": tr(inputs["L1_attn_wk"]),
        "wv1t": tr(inputs["L1_attn_wv"]), "wo1t": tr(inputs["L1_attn_wo"]),
        "wr1t": tr(inputs["L1_ffn_W_r"]), "wk2t": tr(inputs["L1_ffn_W_k"]),
        "wv2t": tr(inputs["L1_ffn_W_v"]),
        "bq0pc": _pcol(inputs["L0_attn_bq"]), "bk0pc": _pcol(inputs["L0_attn_bk"]),
        "bo0pc": _pcol(inputs["L0_attn_bo"]), "bl0pc": _pcol(inputs["L0_attn_bl"]),
        "fb0pc": _pcol(inputs["L0_ffn_bias"]),
        "bq1pc": _pcol(inputs["L1_attn_bq"]), "bk1pc": _pcol(inputs["L1_attn_bk"]),
        "bo1pc": _pcol(inputs["L1_attn_bo"]),
        "murpc": _pcol(inputs["L1_ffn_mu_r"]), "mukpc": _pcol(inputs["L1_ffn_mu_k"]),
        "bv0row": f(inputs["L0_attn_bv"]).reshape(1, D),
        "bv1row": f(inputs["L1_attn_bv"]).reshape(1, D),
        "alpha": f(inputs["L0_ffn_alpha"]).reshape(1, 1),
    }
    return sh


def _launch(nc, in_maps):
    res = run_bass_kernel_spmd(nc, in_maps, core_ids=list(range(8)))
    return [r["y"] for r in res.results]


def kernel(**inputs):
    x = np.asarray(inputs["L0_attn_x"], np.float32)
    sh = _prep_shared(inputs)
    xcores = []
    for c in range(8):
        xs = x[c * NSEQ:(c + 1) * NSEQ]
        xcores.append(np.ascontiguousarray(xs.transpose(2, 0, 1).reshape(D, T)))

    if MODE == "fused":
        nc = build_program([SB_A0, SB_F0, SB_A1, SB_F1])
        in_maps = []
        for c in range(8):
            m = dict(sh)
            m["x"] = xcores[c]
            in_maps.append(m)
        ys = _launch(nc, in_maps)
    else:
        cur = xcores
        for p in (SB_A0, SB_F0, SB_A1, SB_F1):
            nc = build_program([p])
            in_maps = []
            for c in range(8):
                m = {k: sh[k] for k in SB_WEIGHTS[p]}
                m["x"] = cur[c]
                in_maps.append(m)
            cur = _launch(nc, in_maps)
        ys = cur

    out = np.stack([np.asarray(yc, np.float32).reshape(D, NSEQ, S).transpose(1, 2, 0)
                    for yc in ys])
    return np.ascontiguousarray(out.reshape(16, S, D))


# version 05
# speedup vs baseline: 1.8693x; 1.8579x over previous
import numpy as np
from contextlib import ExitStack
import concourse.bass as bass
import concourse.mybir as mybir
from concourse.bass_utils import run_bass_kernel_spmd
from concourse.bass import broadcast_tensor_aps

F32 = mybir.dt.float32
BF16 = mybir.dt.bfloat16
I32 = mybir.dt.int32
AF = mybir.ActivationFunctionType
ALU = mybir.AluOpType
AX = mybir.AxisListType

D = 1024
S = 2048
NSEQ = 2
T = NSEQ * S
NCH = 8
GRP = 512
NG = S // GRP
H = 16
DH = 64
NPAIR = 8
LAT = 256
SCALE = 0.125
ESHIFT = 16.0
EPS = 1e-5
NJC = S // 128

SB_A0, SB_F0, SB_A1, SB_F1 = 0, 1, 2, 3

MODE = "fused"


ROLE_ATTRS = [("sync", "sync"), ("tensor", "tensor"), ("vector", "vector"),
              ("scalar", "scalar"), ("gpsimd", "gpsimd")]


class Ctx:
    def __init__(self, nc, sems):
        self.nc = nc
        self.sem = sems
        self.cnt = {k: 0 for k in sems}

    def phase(self, name, flow):
        base = dict(self.cnt)

        def fenced(r):
            for s in ("pe", "act", "dve", "pool", "dx", "dw"):
                r.wait(r.role, s)
            flow(r)

        with self.nc.Block(name) as blk:
            for role, attr in ROLE_ATTRS:
                def mk(role):
                    def body(eng):
                        self.cnt = dict(base)
                        fenced(R(self, role, eng))
                    return body
                getattr(blk, attr)(mk(role))


class R:
    __slots__ = ("ctx", "role", "eng")

    def __init__(self, ctx, role, eng):
        self.ctx = ctx
        self.role = role
        self.eng = eng

    def _fin(self, inst, inc):
        if inc is not None:
            name, n = inc
            if inst is not None:
                inst.then_inc(self.ctx.sem[name], n)
            self.ctx.cnt[name] += n

    def wait(self, role, sem, val=None):
        v = self.ctx.cnt[sem] if val is None else val
        if v <= 0:
            return
        if self.role == role:
            self.eng.wait_ge(self.ctx.sem[sem], v)

    def mm(self, out, lhsT, rhs, start=True, stop=True, tile_position=None, inc=None):
        inst = None
        if self.role == "tensor":
            inst = self.eng.matmul(out, lhsT, rhs, start=start, stop=stop,
                                   tile_position=tile_position)
        self._fin(inst, inc)

    def act(self, out, in_, func, bias, scale=1.0, inc=None):
        inst = None
        if self.role == "scalar":
            inst = self.eng.activation(out, in_, func, bias=bias, scale=scale)
        self._fin(inst, inc)

    def tt(self, out, in0, in1, op, inc=None):
        inst = None
        if self.role == "vector":
            inst = self.eng.tensor_tensor(out, in0, in1, op=op)
        self._fin(inst, inc)

    def ttb(self, out, in0, in1b, op, inc=None):
        inst = None
        if self.role == "vector":
            b = broadcast_tensor_aps(in1b, in0)[0]
            inst = self.eng.tensor_tensor(out, in0, b, op=op)
        self._fin(inst, inc)

    def ts(self, out, in0, s1, s2, op0, op1=None, inc=None):
        inst = None
        if self.role == "vector":
            if op1 is None:
                inst = self.eng.tensor_scalar(out, in0, s1, None, op0=op0)
            else:
                inst = self.eng.tensor_scalar(out, in0, s1, s2, op0=op0, op1=op1)
        self._fin(inst, inc)

    def cp(self, out, in_, inc=None):
        inst = None
        if self.role == "vector":
            inst = self.eng.tensor_copy(out, in_)
        self._fin(inst, inc)

    def red(self, out, in_, op, absval=False, inc=None):
        inst = None
        if self.role == "vector":
            inst = self.eng.tensor_reduce(out, in_, AX.X, op,
                                          apply_absolute_value=absval)
        self._fin(inst, inc)

    def recip(self, out, in_, inc=None):
        inst = None
        if self.role == "vector":
            inst = self.eng.reciprocal(out, in_)
        self._fin(inst, inc)

    def ms(self, ap, val, inc=None):
        inst = None
        if self.role == "gpsimd":
            inst = self.eng.memset(ap, val)
        self._fin(inst, inc)

    def iota(self, ap, pattern, cm, inc=None):
        inst = None
        if self.role == "gpsimd":
            inst = self.eng.iota(ap, pattern, channel_multiplier=cm,
                                 allow_small_or_imprecise_dtypes=True)
        self._fin(inst, inc)

    def pow_tt(self, out, in0, in1, inc=None):
        inst = None
        if self.role == "gpsimd":
            inst = self.eng.tensor_tensor(out, in0, in1, op=ALU.pow)
        self._fin(inst, inc)

    def dma(self, role, out, in_, inc=None):
        inst = None
        if self.role == role:
            inst = self.eng.dma_start(out=out, in_=in_)
        self._fin(inst, inc)


class NS:
    pass


def alloc_buffers(nc, stack):
    E = stack.enter_context
    B = NS()
    sb = nc.sbuf_tensor
    B.xb = E(sb("xb", [128, NCH, GRP], BF16))
    B.xres = E(sb("xres", [128, NCH, GRP], F32))
    B.xn = E(sb("xn", [128, NCH, S + 1], BF16))
    B.kT = E(sb("kT", [128, NPAIR, S], BF16))
    B.v_s = E(sb("v_s", [128, NJC, H * 65], BF16))
    B.ctab = E(sb("ctab", [128, 2, S], BF16))
    B.qp = E(sb("qp", [128, GRP], BF16))
    B.qtmp = E(sb("qtmp", [128, GRP], BF16))
    B.rtmp = E(sb("rtmp", [128, S], BF16))
    B.expb = E(sb("expb", [128, 4, 1024], BF16))
    B.avu = E(sb("avu", [128, 2, GRP], BF16))
    B.nrmB = E(sb("nrmB", [64, GRP], BF16))
    B.dsb = E(sb("dsb", [65, 2, GRP], F32))
    B.selb = E(sb("selb", [65, 64], F32))
    B.ao = E(sb("ao", [128, NPAIR, GRP], BF16))
    B.yb = E(sb("yb", [128, NCH, GRP], BF16))
    B.t1 = E(sb("t1", [128, NCH, GRP], BF16))
    B.t2 = E(sb("t2", [128, NCH, GRP], BF16))
    B.wsl = E(sb("wsl", [128, 4, NCH, 128], BF16))
    B.Pswp = E(sb("Pswp", [128, 128], BF16))
    B.I64 = E(sb("I64", [64, 64], BF16))
    B.Bv = E(sb("Bv", [128, D], BF16))
    B.rbs = E(sb("rbs", [128, GRP], BF16))
    B.abs_ = E(sb("abs_", [128, GRP], BF16))
    B.cols = E(sb("cols", [128, 128], F32))
    B.rows = E(sb("rows", [1, 3, GRP], F32))
    B.onesb = E(sb("onesb", [1, 128], BF16))
    B.onesf = E(sb("onesf", [1, 128], F32))
    B.negf = E(sb("negf", [1, 128], F32))
    B.onesD = E(sb("onesD", [128, 1], BF16))
    B.onesP = E(sb("onesP", [128, 1], F32))
    B.dcol = E(sb("dcol", [128, 1], F32))
    B.e11 = E(sb("e11", [1, 1], F32))
    B.z11 = E(sb("z11", [1, 1], F32))
    B.scrI = E(sb("scrI", [128, GRP], I32))
    B.latT = B.ctab
    B.cosT = B.ctab[:, 0, :]
    B.sinT = B.ctab[:, 1, :]
    B.fr = B.ao
    B.fk = B.expb[:].rearrange("p a b -> p (a b)").rearrange("p (c t) -> p c t", t=GRP)
    B.wrhs = B.t2
    B.scrA = B.xres[:, 0, :]
    B.scrB = B.xres[:, 1, :]
    B.scrC = B.xres[:, 2, :]
    B.scrW = B.xres[:, 4:6, :].rearrange("p c t -> p (c t)")
    B.scrW2 = B.xres[:, 6:8, :].rearrange("p c t -> p (c t)")
    B.brow = B.rows[0:1, 0:2, :].rearrange("a c t -> a (c t)")
    B.xb2 = B.t2
    ps = nc.psum_tensor
    B.pSC = E(ps("pSC", [128, 1024], F32))
    B.pSC2 = E(ps("pSC2", [128, 1024], F32))
    B.pQ = E(ps("pQ", [128, 512], F32))
    B.pM = E(ps("pM", [128, 512], F32))
    B.pavA = E(ps("pavA", [128, 512], F32))
    B.pavB = E(ps("pavB", [128, 512], F32))
    B.col = {
        "bq0": 0, "bk0": 8, "bo0": 16, "fb": 24, "bq1": 32, "bk1": 40,
        "bo1": 48, "mur": 56, "muk": 64, "murc": 72, "mukc": 80,
        "bl0": 88,
        "alpha": 90, "eshift": 91, "zero": 92, "delta": 93, "absacc": 94,
        "pxorf": 102, "pf": 103,
    }
    return B


def ccol(B, name, w=1):
    o = B.col[name]
    return B.cols[:, o:o + w]


def ccol_off(name):
    return {"bq0": 0, "bk0": 8, "bo0": 16, "fb": 24, "bq1": 32, "bk1": 40,
            "bo1": 48, "mur": 56, "muk": 64, "murc": 72, "mukc": 80,
            "bl0": 88, "alpha": 90, "eshift": 91, "zero": 92, "delta": 93,
            "absacc": 94, "pxorf": 102, "pf": 103}[name]


def B_cols_ap(B):
    return B.cols[:]


def emit_ln(ctx, r, B, src3, dst3, resid=None, store=None, loadres=None):
    r.wait("vector", "dw")
    r.wait("vector", "pe")
    r.tt(B.t1[:], src3, src3, op=ALU.mult, inc=("dve", 1))
    if loadres is not None:
        r.wait("sync", "dve")
        r.dma("sync", out=B.xres[:], in_=loadres, inc=("dx", 16))
    r.wait("tensor", "dw")
    for c in range(NCH):
        r.mm(B.pQ[0:1, :], B.onesD[:], src3[:, c, :], start=(c == 0), stop=(c == NCH - 1),
             inc=("pe", 1) if c == NCH - 1 else None)
    r.wait("tensor", "dve")
    for c in range(NCH):
        r.mm(B.pM[0:1, :], B.onesD[:], B.t1[:, c, :], start=(c == 0), stop=(c == NCH - 1),
             inc=("pe", 1) if c == NCH - 1 else None)
    r.wait("vector", "pe")
    r.cp(B.rows[:, 0, :], B.pQ[0:1, :])
    r.cp(B.rows[:, 1, :], B.pM[0:1, :])
    r.tt(B.rows[:, 2, :], B.rows[:, 0, :], B.rows[:, 0, :], op=ALU.mult)
    r.tt(B.rows[:, 2, :], B.rows[:, 1, :], B.rows[:, 2, :], op=ALU.subtract, inc=("dve", 1))
    r.wait("scalar", "dve")
    r.act(B.rows[:, 2, :], B.rows[:, 2, :], AF.Ln, bias=B.e11[:], scale=1.0)
    r.act(B.rows[:, 2, :], B.rows[:, 2, :], AF.Exp, bias=B.z11[:], scale=-0.5, inc=("act", 1))
    r.wait("vector", "act")
    r.tt(B.rows[:, 1, :], B.rows[:, 0, :], B.rows[:, 2, :], op=ALU.mult, inc=("dve", 1))
    r.wait("tensor", "act")
    r.wait("tensor", "dve")
    r.mm(B.pSC[:, 0:512], B.onesf[:], B.rows[:, 2, :], inc=("pe", 1))
    r.mm(B.pSC[:, 512:1024], B.negf[:], B.rows[:, 1, :], inc=("pe", 1))
    r.wait("vector", "pe")
    r.cp(B.rbs[:], B.pSC[:, 0:512])
    r.cp(B.abs_[:], B.pSC[:, 512:1024], inc=("dve", 1))
    rb = B.rbs[:].rearrange("p (o t) -> p o t", o=1)
    ab = B.abs_[:].rearrange("p (o t) -> p o t", o=1)
    r.ttb(B.t1[:], src3, rb, op=ALU.mult)
    if resid is None:
        r.ttb(dst3, B.t1[:], ab, op=ALU.add, inc=("dve", 1))
    else:
        r.ttb(B.t2[:], B.t1[:], ab, op=ALU.add)
        r.wait("vector", "dx")
        r.tt(B.xres[:], B.xres[:], B.t2[:], op=ALU.add, inc=("dve", 1))
        if store is not None:
            r.wait("sync", "dve")
            r.dma("sync", out=store, in_=B.xres[:], inc=("dx", 16))


def dram_grp(dram, seq, g, w=GRP):
    c0 = seq * S + g * GRP
    xr = dram.rearrange("(c p) t -> p c t", p=128)
    return xr[:, :, c0:c0 + w]


def wslice_ap(wdram, m, kw=NCH):
    wr = wdram.rearrange("(c p) o -> p c o", p=128)
    return wr[:, :, m * 128:(m + 1) * 128]


def emit_kv_phase(ctx, B, seq, L):
    wq = "Wq0" if L == 0 else "wq1t"
    wk = "Wk0" if L == 0 else "wk1t"
    wv = "Wv0" if L == 0 else "wv1t"
    bkc = ccol(B, "bk0" if L == 0 else "bk1", 8)
    nkk = 2 if L == 0 else NCH

    def kv_flow(r):
        if L == 0:
            for m2 in range(2):
                slot = m2 % 4
                r.dma("gpsimd", out=B.wsl[:, slot, :, :], in_=wslice_ap(ctx.dram["Wl0"], m2),
                      inc=("dw", 16))
                r.wait("tensor", "dw")
                for g in range(NG):
                    ps = [B.pSC, B.pSC2][g % 2][:, 0:512]
                    r.wait("tensor", "act", ctx.cnt["act"])
                    for k in range(NCH):
                        r.mm(ps, B.wsl[:, slot, k, :], B.xn[:, k, 1 + g * GRP:1 + (g + 1) * GRP],
                             start=(k == 0), stop=(k == NCH - 1),
                             inc=("pe", 1) if k == NCH - 1 else None)
                    r.wait("scalar", "pe")
                    r.act(B.latT[:, m2, g * GRP:(g + 1) * GRP], ps, AF.Identity,
                          bias=ccol(B, "bl0", 2)[:, m2:m2 + 1], scale=1.0, inc=("act", 1))
        for m in range(NPAIR):
            slot = m % 4
            r.wait("gpsimd", "act")
            r.dma("gpsimd", out=B.wsl[:, slot, :, 0:128] if nkk == NCH else B.wsl[:, slot, 0:nkk, :],
                  in_=wslice_ap(ctx.dram[wk], m, nkk), inc=("dw", 16))
            r.wait("tensor", "dw")
            kev = []
            for g in range(NG):
                ps = [B.pSC, B.pSC2][g % 2][:, 0:512]
                if g < 2:
                    r.wait("tensor", "act", ctx.cnt["act"])
                else:
                    r.wait("tensor", "act", kev[g - 2])
                for k in range(nkk):
                    rhs = (B.latT[:, k, g * GRP:(g + 1) * GRP] if L == 0 else
                           B.xn[:, k, 1 + g * GRP:1 + (g + 1) * GRP])
                    r.mm(ps, B.wsl[:, slot, k, :], rhs, start=(k == 0), stop=(k == nkk - 1),
                         inc=("pe", 1) if k == nkk - 1 else None)
                r.wait("scalar", "pe")
                r.act(B.kT[:, m, g * GRP:(g + 1) * GRP], ps, AF.Identity,
                      bias=bkc[:, m:m + 1], scale=1.0, inc=("act", 1))
                kev.append(ctx.cnt["act"])
            if L == 1:
                row = B.kT[:, m, :]
                r.wait("tensor", "act")
                for gg in range(4):
                    ps = [B.pSC, B.pSC2][gg // 2][:, (gg % 2) * 512:(gg % 2) * 512 + 512]
                    r.mm(ps, B.Pswp[:], row[:, gg * 512:(gg + 1) * 512], inc=("pe", 1))
                r.wait("vector", "act")
                r.tt(B.rtmp[:], row, B.cosT[:], op=ALU.mult)
                r.wait("vector", "pe")
                for gg in range(4):
                    ps = [B.pSC, B.pSC2][gg // 2][:, (gg % 2) * 512:(gg % 2) * 512 + 512]
                    r.tt(row[:, gg * 512:(gg + 1) * 512], ps,
                         B.sinT[:, gg * 512:(gg + 1) * 512], op=ALU.mult)
                r.tt(row, row, B.rtmp[:], op=ALU.add, inc=("dve", 1))
                r.wait("tensor", "dve")
        for half in range(2):
            r.wait("gpsimd", "dve")
            r.wait("gpsimd", "act")
            r.wait("gpsimd", "pe")
            r.dma("gpsimd", out=B.wrhs[:, 0:nkk, :],
                  in_=ctx.dram[wv].rearrange("(c p) o -> p c o", p=128)[:, :, half * 512:(half + 1) * 512],
                  inc=("dw", 16))
            r.wait("tensor", "dw")
            vev = []
            for jc in range(NJC):
                ps = [B.pSC, B.pSC2][jc % 2][:, 0:512]
                if jc < 2:
                    r.wait("tensor", "dve", ctx.cnt["dve"])
                else:
                    r.wait("tensor", "dve", vev[jc - 2])
                for k in range(nkk):
                    lhsT = (B.latT[:, k, jc * 128:(jc + 1) * 128] if L == 0 else
                            B.xn[:, k, 1 + jc * 128:1 + (jc + 1) * 128])
                    r.mm(ps, lhsT, B.wrhs[:, k, :], start=(k == 0), stop=(k == nkk - 1),
                         inc=("pe", 1) if k == nkk - 1 else None)
                r.wait("vector", "pe")
                vv = B.v_s[:, jc, :].rearrange("p (h c) -> p h c", c=65)
                bv = B.Bv[:, half * 512:(half + 1) * 512].rearrange("p (h c) -> p h c", c=64)
                r.tt(vv[:, half * 8:(half + 1) * 8, 0:64],
                     ps.rearrange("p (h c) -> p h c", c=64), bv, op=ALU.add, inc=("dve", 1))
                vev.append(ctx.cnt["dve"])
    ctx.phase(f"kv_a{L}_s{seq}", kv_flow)


def emit_attn(ctx, B, seq, L, src, dst):
    wq = "Wq0" if L == 0 else "wq1t"
    wk = "Wk0" if L == 0 else "wk1t"
    wv = "Wv0" if L == 0 else "wv1t"
    wo = "Wo0" if L == 0 else "wo1t"
    bqc = ccol(B, "bq0" if L == 0 else "bq1", 8)
    bkc = ccol(B, "bk0" if L == 0 else "bk1", 8)
    boc = ccol(B, "bo0" if L == 0 else "bo1", 8)
    nkk = 2 if L == 0 else NCH
    bvrow = "bv0row" if L == 0 else "bv1row"

    if L == 1:
        emit_prep_tables(ctx, B, tag=f"_s{seq}")

    def bv_flow(r):
        r.wait("sync", "dx")
        r.wait("sync", "pe")
        r.wait("sync", "dve")
        r.dma("sync", out=B.brow, in_=ctx.dram[bvrow], inc=("dx", 16))
        r.wait("tensor", "dx")
        r.mm(B.pSC[:, 0:512], B.onesf[:], B.rows[0:1, 0, :], inc=("pe", 1))
        r.mm(B.pSC[:, 512:1024], B.onesf[:], B.rows[0:1, 1, :], inc=("pe", 1))
        r.wait("vector", "pe")
        r.cp(B.Bv[:, 0:512], B.pSC[:, 0:512])
        r.cp(B.Bv[:, 512:1024], B.pSC[:, 512:1024], inc=("dve", 1))
    ctx.phase(f"bv_a{L}_s{seq}", bv_flow)

    def ln1_flow(r):
        ring = [B.xb, B.xb2]
        guard = [(ctx.cnt["pe"], ctx.cnt["dve"])] * 2
        for g in range(NG):
            if g == 0:
                r.wait("gpsimd", "dx")
                r.wait("gpsimd", "pe")
                r.wait("gpsimd", "dve")
                r.dma("gpsimd", out=ring[0][:], in_=dram_grp(ctx.dram[src], seq, 0),
                      inc=("dw", 16))
            if g + 1 < NG:
                pg, dg = guard[(g + 1) % 2]
                r.wait("gpsimd", "pe", pg)
                r.wait("gpsimd", "dve", dg)
                r.dma("gpsimd", out=ring[(g + 1) % 2][:],
                      in_=dram_grp(ctx.dram[src], seq, g + 1), inc=("dw", 16))
            emit_ln(ctx, r, B, ring[g % 2][:], B.xn[:, :, 1 + g * GRP: 1 + (g + 1) * GRP])
            guard[g % 2] = (ctx.cnt["pe"], ctx.cnt["dve"])
    ctx.phase(f"ln1_a{L}_s{seq}", ln1_flow)

    emit_kv_phase(ctx, B, seq, L)

    def attn_flow(r):
        wslot_guard = [ctx.cnt["pe"]] * 4
        wslot_i = [0]

        def stage(g, m, prev, pq_free, pm_free):
            gc = slice(1 + g * GRP, 1 + (g + 1) * GRP)
            tc = slice(g * GRP, (g + 1) * GRP)
            st = {"m": m}
            slot = wslot_i[0] % 4
            wslot_i[0] += 1
            r.wait("gpsimd", "pe", wslot_guard[slot])
            r.dma("gpsimd", out=B.wsl[:, slot, :, :], in_=wslice_ap(ctx.dram[wq], m),
                  inc=("dw", 16))
            r.wait("tensor", "dw")
            r.wait("tensor", "dve", pq_free)
            for k in range(NCH):
                r.mm(B.pQ[:], B.wsl[:, slot, k, :], B.xn[:, k, gc],
                     start=(k == 0), stop=(k == NCH - 1),
                     inc=("pe", 1) if k == NCH - 1 else None)
            wslot_guard[slot] = ctx.cnt["pe"]
            r.wait("scalar", "pe")
            r.act(B.qp[:], B.pQ[:], AF.Identity, bias=bqc[:, m:m + 1], scale=1.0,
                  inc=("act", 1))
            evq = ctx.cnt["act"]
            if L == 1:
                r.wait("tensor", "act", evq)
                r.wait("tensor", "dve", pm_free)
                r.mm(B.pM[:], B.Pswp[:], B.qp[:], inc=("pe", 1))
                r.wait("vector", "act", evq)
                r.tt(B.qtmp[:], B.qp[:], B.cosT[:, tc], op=ALU.mult)
                r.wait("vector", "pe")
                r.tt(B.qp[:], B.pM[:], B.sinT[:, tc], op=ALU.mult)
                r.tt(B.qp[:], B.qp[:], B.qtmp[:], op=ALU.add, inc=("dve", 1))
                r.wait("tensor", "dve", ctx.cnt["dve"])
            else:
                r.wait("tensor", "act", evq)
            rope_done = ctx.cnt["dve"]
            act0 = ctx.cnt["act"]
            av_pe = {}
            for j in range(NJC + 1):
                if j < NJC:
                    psc = [B.pSC, B.pSC2][j % 2]
                    r.wait("tensor", "act", act0 + (j - 1))
                    r.mm(psc[:, 0:512], B.kT[0:64, m, j * 128:(j + 1) * 128],
                         B.qp[0:64, :], tile_position=(0, 0))
                    r.mm(psc[:, 512:1024], B.kT[64:128, m, j * 128:(j + 1) * 128],
                         B.qp[64:128, :], tile_position=(64, 0), inc=("pe", 1))
                    r.wait("scalar", "pe", ctx.cnt["pe"])
                    if j >= 4 and (j - 4) in av_pe:
                        r.wait("scalar", "pe", av_pe[j - 4])
                    r.act(B.expb[:, j % 4, :], psc[:], AF.Exp, bias=ccol(B, "eshift"),
                          scale=SCALE, inc=("act", 1))
                if j == 3 and prev is not None:
                    r.wait("tensor", "dve", max(rope_done, prev["recip_dve"]))
                    r.mm(B.pQ[0:64, :], B.selb[:], B.dsb[:, 0, :], inc=("pe", 1))
                    r.mm(B.pM[0:64, :], B.selb[:], B.dsb[:, 1, :], inc=("pe", 1))
                    prev["bc_pe"] = ctx.cnt["pe"]
                    r.wait("vector", "pe", prev["bc_pe"])
                    r.tt(B.ao[0:64, prev["m"], :], B.avu[0:64, 0, :], B.pQ[0:64, :],
                         op=ALU.mult)
                    r.tt(B.nrmB[:], B.avu[0:64, 1, :], B.pM[0:64, :], op=ALU.mult,
                         inc=("dve", 1))
                    prev["nrm_dve"] = ctx.cnt["dve"]
                if j == 6 and prev is not None:
                    r.wait("tensor", "dve", prev["nrm_dve"])
                    r.mm(B.pQ[64:128, :], B.I64[:], B.nrmB[:], tile_position=(0, 64),
                         inc=("pe", 1))
                    prev["reloc_pe"] = ctx.cnt["pe"]
                ja = j - 1
                if 0 <= ja < NJC:
                    r.wait("tensor", "act", act0 + ja + 1)
                    r.mm(B.pavA[0:65, :], B.v_s[:, ja, (2 * m) * 65:(2 * m) * 65 + 65],
                         B.expb[:, ja % 4, 0:512], start=(ja == 0), stop=(ja == NJC - 1))
                    r.mm(B.pavB[0:65, :],
                         B.v_s[:, ja, (2 * m + 1) * 65:(2 * m + 1) * 65 + 65],
                         B.expb[:, ja % 4, 512:1024], start=(ja == 0),
                         stop=(ja == NJC - 1), inc=("pe", 1))
                    av_pe[ja] = ctx.cnt["pe"]
            av_done = ctx.cnt["pe"]
            if prev is not None:
                r.wait("vector", "pe", prev["reloc_pe"])
                r.cp(B.ao[64:128, prev["m"], :], B.pQ[64:128, :], inc=("dve", 1))
                prev["aoB_dve"] = ctx.cnt["dve"]
            r.wait("vector", "pe", av_done)
            r.cp(B.avu[0:64, 0, :], B.pavA[0:64, :])
            r.cp(B.dsb[64:65, 0, :], B.pavA[64:65, :])
            r.cp(B.avu[0:64, 1, :], B.pavB[0:64, :])
            r.cp(B.dsb[64:65, 1, :], B.pavB[64:65, :])
            r.recip(B.dsb[64:65, 0, :], B.dsb[64:65, 0, :])
            r.recip(B.dsb[64:65, 1, :], B.dsb[64:65, 1, :], inc=("dve", 1))
            st["recip_dve"] = ctx.cnt["dve"]
            return st

        def flush_pair(prev):
            r.wait("tensor", "dve", prev["recip_dve"])
            r.wait("tensor", "act", ctx.cnt["act"])
            r.mm(B.pQ[0:64, :], B.selb[:], B.dsb[:, 0, :], inc=("pe", 1))
            r.mm(B.pM[0:64, :], B.selb[:], B.dsb[:, 1, :], inc=("pe", 1))
            prev["bc_pe"] = ctx.cnt["pe"]
            r.wait("vector", "pe", prev["bc_pe"])
            r.tt(B.ao[0:64, prev["m"], :], B.avu[0:64, 0, :], B.pQ[0:64, :], op=ALU.mult)
            r.tt(B.nrmB[:], B.avu[0:64, 1, :], B.pM[0:64, :], op=ALU.mult,
                 inc=("dve", 1))
            prev["nrm_dve"] = ctx.cnt["dve"]
            r.wait("tensor", "dve", prev["nrm_dve"])
            r.mm(B.pQ[64:128, :], B.I64[:], B.nrmB[:], tile_position=(0, 64),
                 inc=("pe", 1))
            prev["reloc_pe"] = ctx.cnt["pe"]
            r.wait("vector", "pe", prev["reloc_pe"])
            r.cp(B.ao[64:128, prev["m"], :], B.pQ[64:128, :], inc=("dve", 1))
            prev["aoB_dve"] = ctx.cnt["dve"]

        for g in range(NG):
            r.wait("sync", "dve")
            r.wait("sync", "dx")
            r.dma("sync", out=B.xres[:], in_=dram_grp(ctx.dram[src], seq, g), inc=("dx", 16))
            prev = None
            prev2 = None
            base = ctx.cnt["dve"]
            for m in range(NPAIR):
                pq_free = prev2["aoB_dve"] if prev2 is not None else base
                pm_free = prev2["nrm_dve"] if prev2 is not None else base
                cur = stage(g, m, prev, pq_free, pm_free)
                prev2 = prev
                prev = cur
            flush_pair(prev)
            wo_ev = []
            for m2 in range(NCH):
                slot = wslot_i[0] % 4
                wslot_i[0] += 1
                r.wait("gpsimd", "pe", wslot_guard[slot])
                r.dma("gpsimd", out=B.wsl[:, slot, :, :], in_=wslice_ap(ctx.dram[wo], m2),
                      inc=("dw", 16))
                ps = [B.pSC, B.pSC2][m2 % 2][:, 0:512]
                r.wait("tensor", "dw")
                if m2 == 0:
                    r.wait("tensor", "act", ctx.cnt["act"])
                    r.wait("tensor", "dve", ctx.cnt["dve"])
                if m2 >= 2:
                    r.wait("tensor", "act", wo_ev[m2 - 2])
                for k in range(NCH):
                    r.mm(ps, B.wsl[:, slot, k, :], B.ao[:, k, :], start=(k == 0),
                         stop=(k == NCH - 1), inc=("pe", 1) if k == NCH - 1 else None)
                wslot_guard[slot] = ctx.cnt["pe"]
                r.wait("scalar", "pe")
                if m2 == 0:
                    r.wait("scalar", "dve", ctx.cnt["dve"])
                r.act(B.yb[:, m2, :], ps, AF.Identity, bias=boc[:, m2:m2 + 1], scale=1.0,
                      inc=("act", 1))
                wo_ev.append(ctx.cnt["act"])
            r.wait("vector", "act")
            emit_ln(ctx, r, B, B.yb[:], None, resid=True,
                    store=dram_grp(ctx.dram[dst], seq, g))
    ctx.phase(f"attn_a{L}_s{seq}", attn_flow)


def emit_ffn0(ctx, B, seq, src, dst):
    def flow(r):
        f0_guard = [ctx.cnt["pe"]] * 4
        for g in range(NG):
            r.wait("gpsimd", "dx")
            r.wait("gpsimd", "pe")
            r.wait("gpsimd", "dve")
            r.dma("gpsimd", out=B.xb[:], in_=dram_grp(ctx.dram[src], seq, g), inc=("dw", 16))
            emit_ln(ctx, r, B, B.xb[:], B.xn[:, :, 1 + g * GRP:1 + (g + 1) * GRP])
            r.wait("sync", "dve")
            r.wait("sync", "dx")
            r.dma("sync", out=B.xres[:], in_=dram_grp(ctx.dram[src], seq, g), inc=("dx", 16))
            f0_ev = []
            for m2 in range(NCH):
                slot = m2 % 4
                r.wait("sync", "pe", f0_guard[slot])
                r.dma("sync", out=B.wsl[:, slot, :, :], in_=wslice_ap(ctx.dram["tdram"], m2),
                      inc=("dx", 16))
                ps = [B.pSC, B.pSC2][m2 % 2][:, 0:512]
                r.wait("tensor", "dx")
                r.wait("tensor", "act", ctx.cnt["act"])
                r.wait("tensor", "dve", ctx.cnt["dve"])
                for k in range(NCH):
                    r.mm(ps, B.wsl[:, slot, k, :], B.xn[:, k, 1 + g * GRP:1 + (g + 1) * GRP],
                         start=(k == 0), stop=(k == NCH - 1),
                         inc=("pe", 1) if k == NCH - 1 else None)
                f0_guard[slot] = ctx.cnt["pe"]
                r.wait("scalar", "pe")
                if m2 == 0:
                    r.wait("scalar", "dve", ctx.cnt["dve"])
                r.act(B.yb[:, m2, :], ps, AF.Identity, bias=ccol(B, "fb", 8)[:, m2:m2 + 1],
                      scale=ccol(B, "alpha"), inc=("act", 1))
                f0_ev.append(ctx.cnt["act"])
            r.wait("vector", "act")
            emit_ln(ctx, r, B, B.yb[:], None, resid=True,
                    store=dram_grp(ctx.dram[dst], seq, g))
    ctx.phase(f"ffn0_s{seq}", flow)


def emit_ffn1(ctx, B, seq, src, dst):
    def flow(r):
        guard = [ctx.cnt["pe"]] * 4
        sidx = [0]
        for g in range(NG):
            gc = slice(1 + g * GRP, 1 + (g + 1) * GRP)
            pc = slice(g * GRP, g * GRP + GRP)
            r.wait("gpsimd", "dx")
            r.wait("gpsimd", "pe")
            r.wait("gpsimd", "dve")
            r.dma("gpsimd", out=B.xb[:], in_=dram_grp(ctx.dram[src], seq, g), inc=("dw", 16))
            emit_ln(ctx, r, B, B.xb[:], B.xn[:, :, gc])
            r.wait("sync", "dve")
            r.wait("sync", "dx")
            r.dma("sync", out=B.xres[:], in_=dram_grp(ctx.dram[src], seq, g), inc=("dx", 16))
            r.wait("vector", "dve", ctx.cnt["dve"])
            r.wait("vector", "pe", ctx.cnt["pe"])
            for c in range(NCH):
                r.ts(B.t1[:, c, :], B.xn[:, c, gc], ccol(B, "mur", 8)[:, c:c + 1], None,
                     op0=ALU.mult)
                r.ts(B.qtmp[:], B.xn[:, c, pc], ccol(B, "murc", 8)[:, c:c + 1], None,
                     op0=ALU.mult)
                r.tt(B.t1[:, c, :], B.t1[:, c, :], B.qtmp[:], op=ALU.add)
                r.ts(B.t2[:, c, :], B.xn[:, c, gc], ccol(B, "muk", 8)[:, c:c + 1], None,
                     op0=ALU.mult)
                r.ts(B.qtmp[:], B.xn[:, c, pc], ccol(B, "mukc", 8)[:, c:c + 1], None,
                     op0=ALU.mult)
                r.tt(B.t2[:, c, :], B.t2[:, c, :], B.qtmp[:], op=ALU.add,
                     inc=("dve", 1) if c == NCH - 1 else None)

            def mm_loop(wname, rhs3, first_waits, evac):
                evs = []
                for m2 in range(NCH):
                    slot = sidx[0] % 4
                    sidx[0] += 1
                    r.wait("gpsimd", "pe", guard[slot])
                    r.dma("gpsimd", out=B.wsl[:, slot, :, :],
                          in_=wslice_ap(ctx.dram[wname], m2), inc=("dw", 16))
                    ps = [B.pSC, B.pSC2][m2 % 2][:, 0:512]
                    r.wait("tensor", "dw")
                    r.wait("tensor", "dve", ctx.cnt["dve"])
                    r.wait("tensor", "act", ctx.cnt["act"])
                    for k in range(NCH):
                        r.mm(ps, B.wsl[:, slot, k, :], rhs3[:, k, :], start=(k == 0),
                             stop=(k == NCH - 1), inc=("pe", 1) if k == NCH - 1 else None)
                    guard[slot] = ctx.cnt["pe"]
                    evs.append(evac(m2, ps))
                return evs

            def ev_r(m2, ps):
                r.wait("scalar", "pe")
                if m2 == 0:
                    r.wait("scalar", "dve", ctx.cnt["dve"])
                r.act(B.fr[:, m2, :], ps, AF.Sigmoid, bias=ccol(B, "zero"), scale=1.0,
                      inc=("act", 1))
                return ("act", ctx.cnt["act"])
            mm_loop("wr1t", B.t1, [("dve", ctx.cnt["dve"]), ("act", ctx.cnt["act"])], ev_r)

            def ev_k(m2, ps):
                r.wait("vector", "pe")
                r.ts(B.qtmp[:], ps, 0.0, None, op0=ALU.max)
                r.tt(B.fk[:, m2, :], B.qtmp[:], B.qtmp[:], op=ALU.mult, inc=("dve", 1))
                return ("dve", ctx.cnt["dve"])
            mm_loop("wk2t", B.t2, [("act", ctx.cnt["act"])], ev_k)

            r.wait("vector", "act")
            def ev_v(m2, ps):
                r.wait("vector", "pe")
                r.tt(B.yb[:, m2, :], B.fr[:, m2, :], ps, op=ALU.mult, inc=("dve", 1))
                return ("dve", ctx.cnt["dve"])
            mm_loop("wv2t", B.fk, [("dve", ctx.cnt["dve"])], ev_v)

            r.wait("vector", "dve", ctx.cnt["dve"])
            emit_ln(ctx, r, B, B.yb[:], None, resid=True,
                    store=dram_grp(ctx.dram[dst], seq, g))
    ctx.phase(f"ffn1_s{seq}", flow)


def emit_prep_consts(ctx, B, need):
    d = ctx.dram

    def flow(r):
        r.ms(B.onesb[:], 1.0)
        r.ms(B.onesf[:], 1.0)
        r.ms(B.negf[:], -1.0)
        r.ms(B.onesD[:], 1.0 / 1024.0)
        r.ms(B.onesP[:], 1.0)
        r.ms(B.e11[:], EPS)
        r.ms(B.z11[:], 0.0)
        r.ms(B.cols[:], 0.0)
        r.ms(ccol(B, "eshift"), -ESHIFT)
        r.ms(B.xn[:, :, 0:1], 0.0)
        r.ms(B.selb[:], 0.0)
        r.ms(B.selb[64:65, :], 1.0)
        vv = B.v_s[:].rearrange("p j (h c) -> p j h c", c=65)
        r.ms(vv[:, :, :, 64:65], 1.0, inc=("pool", 1))
        r.wait("scalar", "pool")
        for fn in (AF.Identity, AF.Exp, AF.Ln, AF.Sigmoid, AF.Sign, AF.Abs,
                   AF.Sin, AF.Square):
            r.act(B.rows[0:1, 0, 0:1], B.e11[:], fn, bias=B.z11[:], scale=1.0)
        r.act(B.rows[0:1, 0, 0:1], B.e11[:], AF.Identity, bias=B.z11[:], scale=1.0,
              inc=("act", 1))
        r.wait("sync", "pool")
        loads = []
        if SB_A0 in need:
            loads += [("bq0pc", "bq0", 8), ("bk0pc", "bk0", 8), ("bo0pc", "bo0", 8),
                      ("bl0pc", "bl0", 2)]
        if SB_F0 in need:
            loads += [("fb0pc", "fb", 8)]
        if SB_A1 in need:
            loads += [("bq1pc", "bq1", 8), ("bk1pc", "bk1", 8), ("bo1pc", "bo1", 8)]
        if SB_F1 in need:
            loads += [("murpc", "mur", 8), ("mukpc", "muk", 8)]
        for nm, cn, w in loads:
            r.dma("sync", out=ccol(B, cn, w), in_=d[nm], inc=("dx", 16))
        if SB_F0 in need:
            r.dma("sync", out=B.rows[0:1, 0, 0:1], in_=d["alpha"], inc=("dx", 16))
            r.wait("tensor", "dx")
            r.mm(B.pQ[:, 0:1], B.onesf[:], B.rows[0:1, 0, 0:1], inc=("pe", 1))
            r.wait("vector", "pe")
            r.cp(ccol(B, "alpha"), B.pQ[:, 0:1], inc=("dve", 1))
        if SB_F1 in need:
            r.wait("vector", "dx")
            r.ts(ccol(B, "murc", 8), ccol(B, "mur", 8), -1.0, 1.0, op0=ALU.mult, op1=ALU.add)
            r.ts(ccol(B, "mukc", 8), ccol(B, "muk", 8), -1.0, 1.0, op0=ALU.mult, op1=ALU.add,
                 inc=("dve", 1))
        if SB_A1 in need:
            r.iota(B.scrI[:, 0:1], [[1, 1]], cm=1, inc=("pool", 1))
            r.wait("vector", "pool")
            r.ts(B.scrI[:, 1:2], B.scrI[:, 0:1], 32, None, op0=ALU.bitwise_xor)
            r.cp(ccol(B, "pf"), B.scrI[:, 0:1])
            r.cp(ccol(B, "pxorf"), B.scrI[:, 1:2], inc=("dve", 1))
            r.iota(B.rows[0:1, 1, 0:128], [[1, 128]], cm=0, inc=("pool", 1))
            r.wait("tensor", "pool")
            r.wait("tensor", "dve", ctx.cnt["dve"])
            r.mm(B.pM[:, 0:128], B.onesf[:], B.rows[0:1, 1, 0:128], inc=("pe", 1))
            r.wait("vector", "pe")
            r.ts(B.scrA[:, 0:128], B.pM[:, 0:128], ccol(B, "pxorf"), None, op0=ALU.subtract)
            r.tt(B.scrA[:, 0:128], B.scrA[:, 0:128], B.scrA[:, 0:128], op=ALU.mult)
            r.ts(B.scrA[:, 0:128], B.scrA[:, 0:128], 1.0, None, op0=ALU.min)
            r.ts(B.Pswp[:], B.scrA[:, 0:128], -1.0, 1.0, op0=ALU.mult, op1=ALU.add)
            r.ts(B.scrA[:, 0:128], B.pM[:, 0:128], ccol(B, "pf"), None, op0=ALU.subtract)
            r.tt(B.scrA[:, 0:128], B.scrA[:, 0:128], B.scrA[:, 0:128], op=ALU.mult)
            r.ts(B.scrA[:, 0:128], B.scrA[:, 0:128], 1.0, None, op0=ALU.min)
            r.ts(B.scrB[:, 0:128], B.scrA[:, 0:128], -1.0, 1.0, op0=ALU.mult, op1=ALU.add)
            r.cp(B.I64[:], B.scrB[0:64, 0:64], inc=("dve", 1))
        if SB_A0 in need and SB_A1 not in need:
            r.iota(B.scrI[:, 0:1], [[1, 1]], cm=1, inc=("pool", 1))
            r.wait("vector", "pool")
            r.cp(ccol(B, "pf"), B.scrI[:, 0:1], inc=("dve", 1))
            r.iota(B.rows[0:1, 1, 0:128], [[1, 128]], cm=0, inc=("pool", 1))
            r.wait("tensor", "pool")
            r.wait("tensor", "dve", ctx.cnt["dve"])
            r.mm(B.pM[:, 0:128], B.onesf[:], B.rows[0:1, 1, 0:128], inc=("pe", 1))
            r.wait("vector", "pe")
            r.ts(B.scrA[:, 0:128], B.pM[:, 0:128], ccol(B, "pf"), None, op0=ALU.subtract)
            r.tt(B.scrA[:, 0:128], B.scrA[:, 0:128], B.scrA[:, 0:128], op=ALU.mult)
            r.ts(B.scrA[:, 0:128], B.scrA[:, 0:128], 1.0, None, op0=ALU.min)
            r.ts(B.scrB[:, 0:128], B.scrA[:, 0:128], -1.0, 1.0, op0=ALU.mult, op1=ALU.add)
            r.cp(B.I64[:], B.scrB[0:64, 0:64], inc=("dve", 1))
    ctx.phase("prep_consts", flow)


def emit_prep_tables(ctx, B, tag=""):
    LN1E4_32 = float(np.log(10000.0) / 32.0)
    INV2PI = float(1.0 / (2 * np.pi))
    TWOPI = float(2 * np.pi)

    def flow1(r):
        r.wait("vector", "dx")
        r.wait("vector", "dw")
        r.wait("gpsimd", "dx")
        r.iota(B.scrI[:, 0:1], [[1, 1]], cm=1, inc=("pool", 1))
        r.wait("vector", "pool")
        r.ts(B.scrI[:, 2:3], B.scrI[:, 0:1], 32, None, op0=ALU.bitwise_and)
        r.ts(B.scrI[:, 1:2], B.scrI[:, 0:1], 31, None, op0=ALU.bitwise_and)
        r.cp(B.scrB[:, 0:1], B.scrI[:, 2:3])
        r.cp(B.scrA[:, 0:1], B.scrI[:, 1:2], inc=("dve", 1))
        r.wait("scalar", "dve")
        r.act(ccol(B, "absacc"), B.scrA[:, 0:1], AF.Exp, bias=ccol(B, "zero"), scale=-LN1E4_32,
              inc=("act", 1))
        r.wait("vector", "act")
        r.ts(ccol(B, "pxorf"), B.scrB[:, 0:1], 1.0 / 16.0, -1.0, op0=ALU.mult, op1=ALU.add,
             inc=("dve", 1))
    ctx.phase(f"prep_tab1{tag}", flow1)

    def flow2b(r):
        r.iota(B.rows[0:1, 1, :], [[1, GRP]], cm=0, inc=("pool", 1))
        for cc in range(4):
            base = float(cc * GRP)
            sl = slice(cc * GRP, (cc + 1) * GRP)
            r.wait("tensor", "pool")
            r.wait("tensor", "dve", ctx.cnt["dve"])
            r.mm(B.pQ[:], B.onesf[:], B.rows[0:1, 1, :], inc=("pe", 1))
            r.wait("vector", "pe")
            r.ts(B.scrC[:], B.pQ[:], base, None, op0=ALU.add)
            r.ts(B.scrC[:], B.scrC[:], ccol(B, "absacc"), None, op0=ALU.mult)
            r.ts(B.scrB[:], B.scrC[:], INV2PI, None, op0=ALU.mult)
            r.cp(B.scrI[:], B.scrB[:])
            r.cp(B.scrA[:], B.scrI[:])
            r.tt(B.scrB[:], B.scrB[:], B.scrA[:], op=ALU.subtract, inc=("dve", 1))
            r.wait("scalar", "dve")
            r.act(B.scrB[:], B.scrB[:], AF.Sin, bias=ccol(B, "zero"), scale=TWOPI, inc=("act", 1))
            r.wait("vector", "act")
            r.ts(B.sinT[:, sl], B.scrB[:], ccol(B, "pxorf"), None, op0=ALU.mult)
            r.ts(B.scrB[:], B.scrC[:], INV2PI, 0.25, op0=ALU.mult, op1=ALU.add)
            r.cp(B.scrI[:], B.scrB[:])
            r.cp(B.scrA[:], B.scrI[:])
            r.tt(B.scrB[:], B.scrB[:], B.scrA[:], op=ALU.subtract, inc=("dve", 1))
            r.wait("scalar", "dve")
            r.act(B.scrB[:], B.scrB[:], AF.Sin, bias=ccol(B, "zero"), scale=TWOPI, inc=("act", 1))
            r.wait("vector", "act")
            r.cp(B.cosT[:, sl], B.scrB[:], inc=("dve", 1))
    ctx.phase(f"prep_tab2{tag}", flow2b)


def emit_prep_ternary(ctx, B):
    d = ctx.dram
    wr = d["fw0t"].rearrange("(c p) o -> p c o", p=128)
    tr = d["tdram"].rearrange("(c p) o -> p c o", p=128)

    def flow(r):
        for c in range(NCH):
            r.wait("sync", "dve", ctx.cnt["dve"])
            r.dma("sync", out=B.scrW[:], in_=wr[:, c, :], inc=("dx", 16))
            r.wait("vector", "dx")
            r.red(ccol(B, "absacc", 8)[:, c:c + 1], B.scrW[:], ALU.add, absval=True,
                  inc=("dve", 1))
        r.cp(B.scrB[:], B.scrW[:, 0:512])
        r.red(B.dcol[:], ccol(B, "absacc", 8), ALU.add)
        r.cp(B.scrB[:], B.scrW[:, 0:512])
        r.tt(B.scrB[:], B.scrB[:], B.scrB[:], op=ALU.mult, inc=("dve", 1))
        r.wait("tensor", "dve")
        r.mm(B.pQ[0:1, 0:1], B.dcol[:], B.onesP[:], inc=("pe", 1))
        r.wait("vector", "pe")
        r.ts(B.rows[0:1, 0, 0:1], B.pQ[0:1, 0:1], 0.7 / (1024.0 * 1024.0), None,
             op0=ALU.mult, inc=("dve", 1))
        r.wait("tensor", "dve")
        r.mm(B.pQ[:, 0:1], B.onesf[:], B.rows[0:1, 0, 0:1], inc=("pe", 1))
        r.wait("vector", "pe")
        r.cp(ccol(B, "delta"), B.pQ[:, 0:1], inc=("dve", 1))
        for c in range(NCH):
            r.wait("sync", "dx", ctx.cnt["dx"])
            r.wait("sync", "dve", ctx.cnt["dve"])
            r.wait("sync", "dw", ctx.cnt["dw"])
            r.dma("sync", out=B.scrW[:], in_=wr[:, c, :], inc=("dx", 16))
            r.wait("scalar", "dx")
            r.act(B.rtmp[:, 0:1024], B.scrW[:], AF.Sign, bias=ccol(B, "zero"), scale=1.0)
            r.act(B.scrW2[:], B.scrW[:], AF.Abs, bias=ccol(B, "zero"), scale=1.0,
                  inc=("act", 1))
            r.wait("vector", "act")
            r.ts(B.scrW[:], B.scrW2[:], ccol(B, "delta"), 0.0, op0=ALU.subtract,
                 op1=ALU.max, inc=("dve", 1))
            r.wait("scalar", "dve")
            r.act(B.scrW2[:], B.scrW[:], AF.Sign, bias=ccol(B, "zero"), scale=1.0,
                  inc=("act", 1))
            r.wait("vector", "act")
            r.tt(B.scrW[:], B.rtmp[:, 0:1024], B.scrW2[:], op=ALU.mult, inc=("dve", 1))
            r.wait("gpsimd", "dve")
            r.dma("gpsimd", out=tr[:, c, :], in_=B.scrW[:], inc=("dw", 16))
        r.wait("sync", "dw")
    ctx.phase("prep_tern", flow)


WEIGHT_SPECS = {
    "Wq0": [D, D], "Wl0": [D, LAT], "Wk0": [LAT, D], "Wv0": [LAT, D], "Wo0": [D, D],
    "fw0t": [D, D],
    "wq1t": [D, D], "wk1t": [D, D], "wv1t": [D, D], "wo1t": [D, D],
    "wr1t": [D, D], "wk2t": [D, D], "wv2t": [D, D],
}
PCOL_SPECS = {
    "bq0pc": [128, 8], "bk0pc": [128, 8], "bo0pc": [128, 8], "bl0pc": [128, 2],
    "fb0pc": [128, 8], "bq1pc": [128, 8], "bk1pc": [128, 8], "bo1pc": [128, 8],
    "murpc": [128, 8], "mukpc": [128, 8],
}
ROW_SPECS = {"bv0row": [1, D], "bv1row": [1, D], "alpha": [1, 1]}

SB_WEIGHTS = {
    SB_A0: ["Wq0", "Wl0", "Wk0", "Wv0", "Wo0", "bq0pc", "bk0pc", "bo0pc", "bl0pc", "bv0row"],
    SB_F0: ["fw0t", "fb0pc", "alpha"],
    SB_A1: ["wq1t", "wk1t", "wv1t", "wo1t", "bq1pc", "bk1pc", "bo1pc", "bv1row"],
    SB_F1: ["wr1t", "wk2t", "wv2t", "murpc", "mukpc"],
}


def build_program(parts):
    nc = bass.Bass("TRN2", target_bir_lowering=False)
    dram = {}
    dram["x"] = nc.dram_tensor("x", [D, T], F32, kind="ExternalInput").ap()
    dram["y"] = nc.dram_tensor("y", [D, T], F32, kind="ExternalOutput").ap()
    if len(parts) > 1:
        dram["xbuf"] = nc.dram_tensor("xbuf", [D, T], F32, kind="Internal").ap()
    if SB_F0 in parts:
        dram["tdram"] = nc.dram_tensor("tdram", [D, D], BF16, kind="Internal").ap()
    names = set()
    for p in parts:
        names.update(SB_WEIGHTS[p])
    for n in sorted(names):
        if n in WEIGHT_SPECS:
            dram[n] = nc.dram_tensor(n, WEIGHT_SPECS[n], F32, kind="ExternalInput").ap()
        elif n in PCOL_SPECS:
            dram[n] = nc.dram_tensor(n, PCOL_SPECS[n], F32, kind="ExternalInput").ap()
        else:
            dram[n] = nc.dram_tensor(n, ROW_SPECS[n], F32, kind="ExternalInput").ap()

    with ExitStack() as stack:
        E = stack.enter_context
        B = alloc_buffers(nc, stack)
        sems = {}
        for s in ["dx", "dw", "pe", "act", "dve", "pool"]:
            sems[s] = E(nc.semaphore(s))
        ctx = Ctx(nc, sems)
        ctx.dram = dram

        emit_prep_consts(ctx, B, set(parts))
        if SB_F0 in parts:
            emit_prep_ternary(ctx, B)

        for i, p in enumerate(parts):
            src = "x" if i == 0 else "xbuf"
            dst = "y" if i == len(parts) - 1 else "xbuf"
            for seq in range(NSEQ):
                if p == SB_A0:
                    emit_attn(ctx, B, seq, 0, src, dst)
                elif p == SB_F0:
                    emit_ffn0(ctx, B, seq, src, dst)
                elif p == SB_A1:
                    emit_attn(ctx, B, seq, 1, src, dst)
                elif p == SB_F1:
                    emit_ffn1(ctx, B, seq, src, dst)

        def fin(r):
            r.wait("sync", "dx")
            r.wait("sync", "dw")
        ctx.phase("fin", fin)
    return nc


def _pcol(b):
    return np.ascontiguousarray(np.asarray(b, np.float32).reshape(-1, 128).T)


def _prep_shared(inputs):
    f = lambda a: np.ascontiguousarray(np.asarray(a, np.float32))
    tr = lambda a: np.ascontiguousarray(np.asarray(a, np.float32).T)
    sh = {
        "Wq0": f(inputs["L0_attn_Wq"]), "Wl0": f(inputs["L0_attn_Wl"]),
        "Wk0": f(inputs["L0_attn_Wk"]), "Wv0": f(inputs["L0_attn_Wv"]),
        "Wo0": f(inputs["L0_attn_Wo"]),
        "fw0t": tr(inputs["L0_ffn_weight"]),
        "wq1t": tr(inputs["L1_attn_wq"]), "wk1t": tr(inputs["L1_attn_wk"]),
        "wv1t": tr(inputs["L1_attn_wv"]), "wo1t": tr(inputs["L1_attn_wo"]),
        "wr1t": tr(inputs["L1_ffn_W_r"]), "wk2t": tr(inputs["L1_ffn_W_k"]),
        "wv2t": tr(inputs["L1_ffn_W_v"]),
        "bq0pc": _pcol(inputs["L0_attn_bq"]), "bk0pc": _pcol(inputs["L0_attn_bk"]),
        "bo0pc": _pcol(inputs["L0_attn_bo"]), "bl0pc": _pcol(inputs["L0_attn_bl"]),
        "fb0pc": _pcol(inputs["L0_ffn_bias"]),
        "bq1pc": _pcol(inputs["L1_attn_bq"]), "bk1pc": _pcol(inputs["L1_attn_bk"]),
        "bo1pc": _pcol(inputs["L1_attn_bo"]),
        "murpc": _pcol(inputs["L1_ffn_mu_r"]), "mukpc": _pcol(inputs["L1_ffn_mu_k"]),
        "bv0row": f(inputs["L0_attn_bv"]).reshape(1, D),
        "bv1row": f(inputs["L1_attn_bv"]).reshape(1, D),
        "alpha": f(inputs["L0_ffn_alpha"]).reshape(1, 1),
    }
    return sh


def _launch(nc, in_maps):
    res = run_bass_kernel_spmd(nc, in_maps, core_ids=list(range(8)))
    return [r["y"] for r in res.results]


def kernel(**inputs):
    x = np.asarray(inputs["L0_attn_x"], np.float32)
    sh = _prep_shared(inputs)
    xcores = []
    for c in range(8):
        xs = x[c * NSEQ:(c + 1) * NSEQ]
        xcores.append(np.ascontiguousarray(xs.transpose(2, 0, 1).reshape(D, T)))

    if MODE == "fused":
        nc = build_program([SB_A0, SB_F0, SB_A1, SB_F1])
        in_maps = []
        for c in range(8):
            m = dict(sh)
            m["x"] = xcores[c]
            in_maps.append(m)
        ys = _launch(nc, in_maps)
    else:
        cur = xcores
        for p in (SB_A0, SB_F0, SB_A1, SB_F1):
            nc = build_program([p])
            in_maps = []
            for c in range(8):
                m = {k: sh[k] for k in SB_WEIGHTS[p]}
                m["x"] = cur[c]
                in_maps.append(m)
            cur = _launch(nc, in_maps)
        ys = cur

    out = np.stack([np.asarray(yc, np.float32).reshape(D, NSEQ, S).transpose(1, 2, 0)
                    for yc in ys])
    return np.ascontiguousarray(out.reshape(16, S, D))
